# version 19
# baseline (speedup 1.0000x reference)
import numpy as np
import ml_dtypes
import concourse.bass as bass
import concourse.mybir as mybir
from concourse.alu_op_type import AluOpType as OP
from concourse.bass_utils import run_bass_kernel_spmd
from contextlib import ExitStack

F32 = mybir.dt.float32
BF16 = mybir.dt.bfloat16
AF = mybir.ActivationFunctionType


def mk_cfg(D=4096, TO=1024, NH=32, NKV=8, E=8, HCH=512, OCH=256):
    c = {}
    c["D"] = D; c["DT"] = D // 128
    c["TO"] = TO; c["NTO"] = TO // 128
    c["TB"] = 2 * TO; c["NTB"] = 2 * TO // 128
    c["NH"] = NH; c["NKV"] = NKV; c["GQ"] = NH // NKV
    c["KD"] = NKV * 128
    c["E"] = E
    c["HCH"] = HCH; c["NCH"] = D // HCH
    c["OCH"] = OCH; c["NOCH"] = D // OCH
    c["KSW"] = c["KD"] // 2
    c["NKVS"] = NKV // 2
    c["NQT"] = TO // 512
    assert NH * 128 == D and TO % 512 == 0 and NKV % 2 == 0
    return c


class Chan:
    def __init__(self, nc, name):
        self.h = nc.alloc_semaphore(name)
        self.n = 0

    def inc(self, instr, k=1):
        instr.then_inc(self.h, k)
        self.n += k
        return self.n

    def wait(self, eng, val=None):
        eng.wait_ge(self.h, self.n if val is None else val)


def build_program(c):
    nc = bass.Bass("TRN2", target_bir_lowering=False)
    D, DT, TO, NTO, TB, NTB = c["D"], c["DT"], c["TO"], c["NTO"], c["TB"], c["NTB"]
    KD, NH, NKV, GQ, E = c["KD"], c["NH"], c["NKV"], c["GQ"], c["E"]
    HCH, NCH, OCH, NOCH = c["HCH"], c["NCH"], c["OCH"], c["NOCH"]
    KSW, NKVS, NQT = c["KSW"], c["NKVS"], c["NQT"]

    dt = nc.dram_tensor
    xo = dt("xo", [TO, D], F32, kind="ExternalInput").ap()
    xoc = dt("xoc", [NCH, 128, NTO, HCH], F32, kind="ExternalInput").ap()
    xb = dt("xb", [TO, D], F32, kind="ExternalInput").ap()
    wq = dt("wq", [NH, 128, DT, 128], F32, kind="ExternalInput").ap()
    wk = dt("wk", [2, 128, DT, KSW], F32, kind="ExternalInput").ap()
    wv = dt("wv", [2, 128, DT, KSW], F32, kind="ExternalInput").ap()
    wo = dt("wo", [NOCH, 128, DT, OCH], F32, kind="ExternalInput").ap()
    we = dt("we", [NCH, E, 128, DT, HCH], F32, kind="ExternalInput").ap()
    wg = dt("wg", [128, DT, E], F32, kind="ExternalInput").ap()
    bqD = dt("bq", [128, NH], F32, kind="ExternalInput").ap()
    bkD = dt("bk", [1, KD], F32, kind="ExternalInput").ap()
    bvD = dt("bv", [1, KD], F32, kind="ExternalInput").ap()
    boD = dt("bo", [1, D], F32, kind="ExternalInput").ap()
    bgD = dt("bg", [E, 1], F32, kind="ExternalInput").ap()
    beD = dt("be", [NCH, E * HCH], F32, kind="ExternalInput").ap()
    out = dt("out", [TO, D], F32, kind="ExternalOutput").ap()
    ascr = dt("ascr", [NOCH, 128, NTO, OCH], BF16, kind="Internal").ap()

    id_bf_h = nc.inline_tensor(np.eye(128, dtype=ml_dtypes.bfloat16), name="id_bf_h")
    id_f_h = nc.inline_tensor(np.eye(E, dtype=np.float32), name="id_f_h")

    PERS = ExitStack()
    S = lambda name, shape, dty: PERS.enter_context(nc.sbuf_tensor(name, shape, dty)).ap()
    with PERS:
        xnt = S("xnt", [128, DT * TO], BF16)
        kvt = S("kvt", [128, NKV * 128], BF16)
        rnorm = S("rnorm", [128, NKV], F32)
        logits = S("logits", [128, NTO * E], F32)
        mask = S("mask", [128, NTO * E], F32)
        ones_bf = S("ones_bf", [128, 128], BF16)
        idbf = S("idbf", [128, 128], BF16)
        idf = S("idf", [E, E], F32)
        wg_f = S("wg_f", [128, DT * E], F32)
        wg_hi = S("wg_hi", [128, DT * E], BF16)
        wg_lo = S("wg_lo", [128, DT * E], BF16)
        bg_c = S("bg_c", [E, 1], F32)
        bqc = S("bqc", [128, NH], F32)
        bkb = S("bkb", [128, KD], BF16)
        bvb = S("bvb", [128, KD], BF16)
        st6 = S("st6", [128, 6 * (D // 512)], F32)
        st2 = S("st2", [128, 2], F32)
        mv = S("mv", [128, 4], F32)
        t8 = S("t8", [128, 8], F32)

        SXA = [Chan(nc, "SXA0"), Chan(nc, "SXA1")]
        SXB = [Chan(nc, "SXB0"), Chan(nc, "SXB1")]
        SXF = [Chan(nc, "SXF0"), Chan(nc, "SXF1")]
        SOUT = [Chan(nc, "SOUT0"), Chan(nc, "SOUT1")]
        SPRE = Chan(nc, "SPRE")
        SKV = Chan(nc, "SKV")
        SQ = [Chan(nc, "SQ0"), Chan(nc, "SQ1")]
        SBQ = Chan(nc, "SBQ")
        SO = [Chan(nc, "SO0"), Chan(nc, "SO1")]
        SBO = Chan(nc, "SBO")
        SE = [Chan(nc, "SE0"), Chan(nc, "SE1")]
        SBE = Chan(nc, "SBE")
        STR = [Chan(nc, "STR0"), Chan(nc, "STR1")]
        SKP = [Chan(nc, f"SKP{i}") for i in range(4)]
        SVP = [Chan(nc, f"SVP{i}") for i in range(4)]
        SOP = [Chan(nc, f"SOP{i}") for i in range(4)]
        SEP = [Chan(nc, f"SEP{i}") for i in range(4)]
        SP3 = Chan(nc, "SP3")
        SAT = Chan(nc, "SAT")
        pvV = Chan(nc, "pvV")
        pvA = Chan(nc, "pvA")
        gT = Chan(nc, "gT")

        def V(v, instr):
            return pvV.inc(instr)

        def A(s, instr):
            return pvA.inc(instr)

        with ExitStack() as PH:
            Sp = lambda n_, sh, dty: PH.enter_context(nc.sbuf_tensor(n_, sh, dty)).ap()
            xbufs = [Sp("xbufA", [128, D], F32), Sp("xbufB", [128, D], F32)]
            xnfs = [Sp("xnfA", [128, D], F32), Sp("xnfB", [128, D], F32)]
            xhis = [Sp("xhiA", [128, D], BF16), Sp("xhiB", [128, D], BF16)]
            xlos = [Sp("xloA", [128, D], BF16), Sp("xloB", [128, D], BF16)]
            lots = [Sp("lotA", [128, D], BF16), Sp("lotB", [128, D], BF16)]
            glog_all = Sp("glog_all", [E, NTO * 128], F32)
            ptH = [PH.enter_context(nc.psum_tensor(f"ptH{i}", [128, 128], BF16)).ap() for i in range(2)]
            ptL = [PH.enter_context(nc.psum_tensor(f"ptL{i}", [128, 128], BF16)).ap() for i in range(2)]
            pG = [PH.enter_context(nc.psum_tensor(f"pG{i}", [E, 128], F32)).ap() for i in range(2)]
            pGT = [PH.enter_context(nc.psum_tensor(f"pGT{i}", [128, E], F32)).ap() for i in range(2)]

            with nc.Block("a0") as blk:
                def prelude_g(g):
                    SPRE.inc(g.dma_start(out=idbf, in_=id_bf_h.ap()), 16)
                    SPRE.inc(g.dma_start(out=idf, in_=id_f_h.ap()), 16)
                    SPRE.inc(g.dma_start(out=wg_f, in_=wg.rearrange("p t e -> p (t e)")), 16)
                    SPRE.inc(g.dma_start(out=bg_c, in_=bgD), 16)
                blk.gpsimd(prelude_g)
                consts_rdy = SPRE.n

                def prelude_v(v):
                    SPRE.wait(v, consts_rdy)
                    V(v, v.memset(ones_bf, 1.0))
                    pvV.wait(v)
                    V(v, v.tensor_copy(wg_hi, wg_f))
                    pvV.wait(v)
                    V(v, v.tensor_tensor(wg_lo, wg_f, wg_hi, OP.subtract))
                blk.vector(prelude_v)
                wg_rdy = pvV.n

                xbuf_free = [0, 0]
                xn_free = [0, 0]
                gate_done = [0] * NTO
                pG_free = 0
                NB = D // 512

                for tau in range(NTO):
                    xbuf = xbufs[tau % 2]
                    xnf, xhi, xlo, lot = xnfs[tau % 2], xhis[tau % 2], xlos[tau % 2], lots[tau % 2]
                    def f_sx(s, tau=tau, xf=xbuf_free[tau % 2], xbuf=xbuf):
                        pvV.wait(s, xf)
                        SXA[tau % 2].inc(s.dma_start(out=xbuf, in_=xo[tau * 128:(tau + 1) * 128, :]), 16)
                    blk.sync(f_sx)
                    x_rdy = SXA[tau % 2].n

                    def f_ln1(v, x_rdy=x_rdy, xbuf=xbuf, tau=tau):
                        SXA[tau % 2].wait(v, x_rdy)
                        for i in range(NB):
                            pvV.wait(v)
                            V(v, v.bn_stats(st6[:, 6 * i:6 * (i + 1)], xbuf[:, 512 * i:512 * (i + 1)]))
                        pvV.wait(v)
                        V(v, v.bn_aggr(st2, st6))
                        pvV.wait(v)
                        V(v, v.tensor_scalar(mv[:, 2:3], st2[:, 1:2], 1e-5, None, OP.add))
                    blk.vector(f_ln1)
                    c_ln1 = pvV.n

                    def f_sq(s, c=c_ln1):
                        pvV.wait(s, c)
                        A(s, s.activation(mv[:, 3:4], mv[:, 2:3], AF.Ln))
                        pvA.wait(s)
                        A(s, s.activation(mv[:, 1:2], mv[:, 3:4], AF.Exp, scale=-0.5))
                    blk.scalar(f_sq)
                    c_sqrt = pvA.n

                    def f_ln2(v, c=c_sqrt, xnfree=xn_free[tau % 2], xbuf=xbuf,
                              xnf=xnf, xhi=xhi, xlo=xlo):
                        pvA.wait(v, c)
                        gT.wait(v, xnfree)
                        V(v, v.tensor_scalar(xnf, xbuf, st2[:, 0:1], mv[:, 1:2], OP.subtract, OP.mult))
                        pvV.wait(v)
                        V(v, v.tensor_copy(xhi, xnf))
                        pvV.wait(v)
                        V(v, v.tensor_tensor(xlo, xnf, xhi, OP.subtract))
                    blk.vector(f_ln2)
                    xbuf_free[tau % 2] = pvV.n
                    c_xn = pvV.n

                    baseA = pvA.n
                    th = [0] * DT
                    tl = [0] * DT

                    def f_tr(t, tau=tau, c_xn=c_xn, baseA=baseA, xhi=xhi, xlo=xlo):
                        pvV.wait(t, c_xn)
                        for j in range(DT):
                            if j >= 2:
                                t.wait_ge(pvA.h, baseA + 2 * j - 3)
                            th[j] = gT.inc(t.transpose(ptH[j % 2], xhi[:, j * 128:(j + 1) * 128], idbf))
                            if j >= 2:
                                t.wait_ge(pvA.h, baseA + 2 * j - 2)
                            tl[j] = gT.inc(t.transpose(ptL[j % 2], xlo[:, j * 128:(j + 1) * 128], idbf))
                    blk.tensor(f_tr)
                    xn_free[tau % 2] = gT.n

                    def f_ev(s, tau=tau, lot=lot):
                        for j in range(DT):
                            gT.wait(s, th[j])
                            A(s, s.copy(xnt[:, j * TO + tau * 128: j * TO + (tau + 1) * 128], ptH[j % 2]))
                            gT.wait(s, tl[j])
                            A(s, s.copy(lot[:, j * 128:(j + 1) * 128], ptL[j % 2]))
                    blk.scalar(f_ev)
                    assert pvA.n == baseA + 2 * DT
                    ev_done_A = pvA.n
                    ev_done_V = pvV.n

                    def f_gate(t, tau=tau, a=ev_done_A, vv=ev_done_V, pf=pG_free, lot=lot):
                        pvA.wait(t, max(a, pf))
                        pvV.wait(t, max(vv, wg_rdy))
                        SPRE.wait(t, consts_rdy)
                        for d in range(DT):
                            xs = xnt[:, d * TO + tau * 128: d * TO + (tau + 1) * 128]
                            ls = lot[:, d * 128:(d + 1) * 128]
                            pg = pG[tau % 2]
                            t.matmul(pg, wg_hi[:, d * E:(d + 1) * E], xs, start=(d == 0), stop=False)
                            t.matmul(pg, wg_lo[:, d * E:(d + 1) * E], xs, start=False, stop=False)
                            last = t.matmul(pg, wg_hi[:, d * E:(d + 1) * E], ls, start=False,
                                            stop=(d == DT - 1))
                        gT.inc(last)
                    blk.tensor(f_gate)
                    gate_done[tau] = gT.n

                    def emit_gcopy(tau2):
                        def f_gc(s, tau2=tau2, c=gate_done[tau2]):
                            gT.wait(s, c)
                            A(s, s.copy(glog_all[:, tau2 * 128:(tau2 + 1) * 128], pG[tau2 % 2]))
                        blk.scalar(f_gc)
                        return pvA.n
                    if tau >= 1:
                        gc = emit_gcopy(tau - 1)
                        if tau >= 1:
                            pG_free_next = gc
                    if tau >= 1:
                        pG_free = pG_free_prev if tau >= 2 else 0
                        pG_free_prev = pG_free_next
                pG_free = emit_gcopy(NTO - 1)
                gcopies = pvA.n

                def f_bg(v, c=gcopies):
                    pvA.wait(v, c)
                    V(v, v.tensor_scalar(glog_all, glog_all, bg_c, None, OP.add))
                blk.vector(f_bg)
                c_bg = pvV.n

                gtr = [0] * NTO
                def f_gtr_all(t, c=c_bg):
                    pvV.wait(t, c)
                    for tau in range(NTO):
                        if tau >= 2:
                            t.wait_ge(pvV.h, c + 3 * (tau - 1))
                        gtr[tau] = gT.inc(t.transpose(pGT[tau % 2],
                                                      glog_all[:, tau * 128:(tau + 1) * 128],
                                                      idf))
                blk.tensor(f_gtr_all)

                def f_msk_all(v, c=c_bg):
                    for tau in range(NTO):
                        gT.wait(v, gtr[tau])
                        V(v, v.tensor_copy(logits[:, tau * E:(tau + 1) * E], pGT[tau % 2]))
                        pvV.wait(v)
                        V(v, v.max(out=t8, in_=logits[:, tau * E:(tau + 1) * E]))
                        pvV.wait(v)
                        V(v, v.tensor_scalar(mask[:, tau * E:(tau + 1) * E],
                                             logits[:, tau * E:(tau + 1) * E],
                                             t8[:, 1:2], 0.5, OP.is_ge, OP.mult))
                    assert pvV.n == c + 3 * NTO
                blk.vector(f_msk_all)
            mask_done = pvV.n

        with ExitStack() as PH:
            Sp = lambda n_, sh, dty: PH.enter_context(nc.sbuf_tensor(n_, sh, dty)).ap()
            wkbf = Sp("wkbf", [128, DT * KSW], BF16)
            wvbf = Sp("wvbf", [128, DT * KSW], BF16)
            xbuf1s = [Sp("xbuf1A", [128, D], F32), Sp("xbuf1B", [128, D], F32)]
            xhi1s = [Sp("xhi1A", [128, D], BF16), Sp("xhi1B", [128, D], BF16)]
            xntt = Sp("xntt", [128, D], BF16)
            ktoks = [Sp("ktokA", [128, KSW], BF16), Sp("ktokB", [128, KSW], BF16)]
            vtoks = [Sp("vtokA", [128, KSW], BF16), Sp("vtokB", [128, KSW], BF16)]
            scr1 = Sp("scr1", [128, KSW], F32)
            scr3 = Sp("scr3", [128, KSW], F32)
            scrK = Sp("scrK", [128, NKVS], F32)
            bkr = Sp("bkr", [1, KD], BF16)
            bvr = Sp("bvr", [1, KD], BF16)
            pK = [PH.enter_context(nc.psum_tensor(f"pK{i}", [128, KSW], F32)).ap() for i in range(2)]
            pV = [PH.enter_context(nc.psum_tensor(f"pV{i}", [128, KSW], F32)).ap() for i in range(2)]
            pKV = PH.enter_context(nc.psum_tensor("pKV", [128, NKVS * 128], F32)).ap()
            pKS = PH.enter_context(nc.psum_tensor("pKS", [128, NKVS], F32)).ap()
            ptC = [PH.enter_context(nc.psum_tensor(f"ptC{i}", [128, 128], BF16)).ap() for i in range(2)]

            with nc.Block("a1") as blk:
                def f_b(g):
                    SKV.inc(g.dma_start(out=bkr, in_=bkD), 16)
                    SKV.inc(g.dma_start(out=bvr, in_=bvD), 16)
                blk.gpsimd(f_b)

                CWb = min(512, KSW)
                bb_chunks = [(off, min(CWb, KD - off)) for off in range(0, KD, CWb)]

                def f_bb(t):
                    SKV.wait(t, 32)
                    SPRE.wait(t, consts_rdy)
                    for i, (off, w) in enumerate(bb_chunks):
                        t.matmul(pK[i % 2][:, 0:w], ones_bf[0:1, 0:128],
                                 bkr[0:1, off:off + w], start=True, stop=True)
                        gT.inc(t.matmul(pV[i % 2][:, 0:w], ones_bf[0:1, 0:128],
                                        bvr[0:1, off:off + w], start=True, stop=True))
                blk.tensor(f_bb)
                c_bbmm = gT.n

                def f_bbe(s):
                    gT.wait(s, c_bbmm)
                    for i, (off, w) in enumerate(bb_chunks):
                        pvA.wait(s)
                        A(s, s.copy(bkb[:, off:off + w], pK[i % 2][:, 0:w]))
                        pvA.wait(s)
                        A(s, s.copy(bvb[:, off:off + w], pV[i % 2][:, 0:w]))
                blk.scalar(f_bbe)
                c_bbe = pvA.n

                kw_free = 0
                kvps_free = 0
                xbuf1_free = [0, 0]
                xhi1_free = [(0, 0), (0, 0)]
                ktok_free = [0, 0]
                c_max_prev = 0
                c_scr1_prev = 0
                pKfree = [(c_bbe, 0)] * 2

                for sw in range(2):
                    PC = DT // 4
                    def f_w(g, sw=sw, kwf=kw_free):
                        gT.wait(g, kwf)
                        for p in range(4):
                            SKP[p].inc(g.dma_start(
                                out=wkbf[:, p * PC * KSW:(p + 1) * PC * KSW],
                                in_=wk[sw][:, p * PC:(p + 1) * PC, :].rearrange("p t f -> p (t f)")), 16)
                            SVP[p].inc(g.dma_start(
                                out=wvbf[:, p * PC * KSW:(p + 1) * PC * KSW],
                                in_=wv[sw][:, p * PC:(p + 1) * PC, :].rearrange("p t f -> p (t f)")), 16)
                    blk.gpsimd(f_w)
                    w_rdy = [(SKP[p].n, SVP[p].n) for p in range(4)]

                    def emit_ln(ot):
                        xbuf1 = xbuf1s[ot % 2]
                        xhi1 = xhi1s[ot % 2]

                        def f_sx(s, ot=ot, xf=xbuf1_free[ot % 2], xbuf1=xbuf1):
                            pvV.wait(s, xf)
                            SXB[ot % 2].inc(s.dma_start(out=xbuf1, in_=xb[ot * 128:(ot + 1) * 128, :]), 16)
                        blk.sync(f_sx)
                        x_rdy = SXB[ot % 2].n

                        def f_ln1(v, x_rdy=x_rdy, xbuf1=xbuf1, ot=ot):
                            SXB[ot % 2].wait(v, x_rdy)
                            for i in range(D // 512):
                                pvV.wait(v)
                                V(v, v.bn_stats(st6[:, 6 * i:6 * (i + 1)], xbuf1[:, 512 * i:512 * (i + 1)]))
                            pvV.wait(v)
                            V(v, v.bn_aggr(st2, st6))
                            pvV.wait(v)
                            V(v, v.tensor_scalar(mv[:, 2:3], st2[:, 1:2], 1e-5, None, OP.add))
                        blk.vector(f_ln1)
                        c_ln1 = pvV.n

                        def f_sq(s, ccc=c_ln1):
                            pvV.wait(s, ccc)
                            A(s, s.activation(mv[:, 3:4], mv[:, 2:3], AF.Ln))
                            pvA.wait(s)
                            A(s, s.activation(mv[:, 1:2], mv[:, 3:4], AF.Exp, scale=-0.5))
                        blk.scalar(f_sq)
                        c_sqrt = pvA.n

                        def f_ln2(v, ccc=c_sqrt, xhf=xhi1_free[ot % 2], xbuf1=xbuf1, xhi1=xhi1):
                            pvA.wait(v, ccc)
                            STR[0].wait(v, xhf[0])
                            STR[1].wait(v, xhf[1])
                            V(v, v.tensor_scalar(xhi1, xbuf1, st2[:, 0:1], mv[:, 1:2],
                                                 OP.subtract, OP.mult))
                        blk.vector(f_ln2)
                        xbuf1_free[ot % 2] = pvV.n
                        return pvV.n

                    other_kv_prev = [0]

                    def emit_tr(ot, c_xn):
                        xhi1 = xhi1s[ot % 2]
                        kvp = other_kv_prev[0]

                        def f_t0(s, c_xn=c_xn, kvp=kvp, xhi1=xhi1):
                            pvV.wait(s, c_xn)
                            gT.wait(s, kvp)
                            for j in range(0, DT, 2):
                                STR[0].inc(s.dma_start_transpose(
                                    xntt[:, j * 128:(j + 1) * 128],
                                    xhi1[:, j * 128:(j + 1) * 128]), 16)
                        blk.sync(f_t0)

                        def f_t1(sc, c_xn=c_xn, kvp=kvp, xhi1=xhi1):
                            pvV.wait(sc, c_xn)
                            gT.wait(sc, kvp)
                            for j in range(1, DT, 2):
                                STR[1].inc(sc.dma_start_transpose(
                                    xntt[:, j * 128:(j + 1) * 128],
                                    xhi1[:, j * 128:(j + 1) * 128]), 16)
                        blk.scalar(f_t1)
                        xhi1_free[ot % 2] = (STR[0].n, STR[1].n)
                        return (STR[0].n, STR[1].n)

                    ln_done = {}
                    for idx, bt in enumerate([x for pr in zip(range(NTO), range(NTO, NTB)) for x in pr]):
                        own = bt < NTO
                        if own and bt < NTO:
                            ln_done[bt] = emit_ln(bt)
                        if not own:
                            xntt_rdy = emit_tr(bt - NTO, ln_done[bt - NTO])
                        else:
                            xntt_rdy = (0, 0)
                        par = idx % 2

                        def f_kv(t, bt=bt, own=own, par=par, sw=sw, w_rdy=w_rdy,
                                 xntt_rdy=xntt_rdy, slot=pKfree[par], PC=PC):
                            if not own:
                                STR[0].wait(t, xntt_rdy[0])
                                STR[1].wait(t, xntt_rdy[1])
                            t.wait_ge(pvA.h, slot[0])
                            t.wait_ge(pvV.h, slot[1])
                            for d in range(DT):
                                if d % PC == 0:
                                    SKP[d // PC].wait(t, w_rdy[d // PC][0])
                                    SVP[d // PC].wait(t, w_rdy[d // PC][1])
                                if own:
                                    xs = xnt[:, d * TO + bt * 128: d * TO + (bt + 1) * 128]
                                else:
                                    xs = xntt[:, d * 128:(d + 1) * 128]
                                t.matmul(pK[par], xs, wkbf[:, d * KSW:(d + 1) * KSW],
                                         start=(d == 0), stop=(d == DT - 1))
                                last = t.matmul(pV[par], xs, wvbf[:, d * KSW:(d + 1) * KSW],
                                                start=(d == 0), stop=(d == DT - 1))
                            gT.inc(last)
                        blk.tensor(f_kv)
                        c_kv = gT.n
                        if not own:
                            other_kv_prev[0] = gT.n
                        if idx == NTB - 1:
                            kw_free = gT.n

                        ktok = ktoks[idx % 2]
                        vtok = vtoks[idx % 2]

                        def f_phi_v1(v, par=par, sw=sw, ckv=c_kv, cbe=c_bbe, ps1=c_scr1_prev):
                            gT.wait(v, ckv)
                            pvA.wait(v, max(cbe, ps1))
                            pvV.wait(v)
                            V(v, v.scalar_tensor_tensor(scr3, pK[par], 1.0,
                                                        bkb[:, sw * KSW:(sw + 1) * KSW],
                                                        OP.add, OP.add))
                        blk.vector(f_phi_v1)
                        c_s3 = pvV.n

                        def f_phi_a(s, cs3=c_s3, cmp_=c_max_prev):
                            pvV.wait(s, cs3)
                            pvV.wait(s, cmp_)
                            A(s, s.activation(scr1, scr3, AF.Relu, scale=-1.0, bias=1.0))
                            pvA.wait(s)
                            A(s, s.activation(scr1, scr1, AF.Exp, scale=-1.0))
                        blk.scalar(f_phi_a)
                        c_phi_a = pvA.n
                        c_scr1_prev = pvA.n - 1

                        def f_phi_v(v, par=par, sw=sw, ckv=c_kv, ca=c_phi_a,
                                    ktf=ktok_free[idx % 2], ktok=ktok, vtok=vtok):
                            gT.wait(v, ktf)
                            pvA.wait(v, ca)
                            pvV.wait(v)
                            V(v, v.tensor_tensor(ktok, scr1, scr3, OP.max))
                            pvV.wait(v)
                            V(v, v.tensor_tensor(vtok, pV[par], bvb[:, sw * KSW:(sw + 1) * KSW], OP.add))
                        blk.vector(f_phi_v)
                        c_tok = pvV.n
                        c_max_prev = pvV.n - 1
                        pKfree[par] = (c_phi_a, c_tok)

                        def f_kvmm(t, idx=idx, ct=c_tok, kvf=kvps_free, ktok=ktok, vtok=vtok):
                            pvV.wait(t, max(ct, kvf))
                            for h in range(NKVS):
                                ks = ktok[:, h * 128:(h + 1) * 128]
                                vs = vtok[:, h * 128:(h + 1) * 128]
                                t.matmul(pKV[:, h * 128:(h + 1) * 128], ks, vs,
                                         start=(idx == 0 and h == 0),
                                         stop=(idx == NTB - 1 and h == NKVS - 1),
                                         skip_group_check=True)
                                last = t.matmul(pKS[:, h:h + 1], ks, ones_bf[:, 0:1],
                                                start=(idx == 0 and h == 0),
                                                stop=(idx == NTB - 1 and h == NKVS - 1),
                                                skip_group_check=True)
                            gT.inc(last)
                        blk.tensor(f_kvmm)
                        ktok_free[idx % 2] = gT.n

                    def f_kvev(v, sw=sw, c=gT.n):
                        gT.wait(v, c)
                        V(v, v.tensor_copy(kvt[:, sw * NKVS * 128:(sw + 1) * NKVS * 128], pKV))
                        pvV.wait(v)
                        V(v, v.tensor_scalar(scrK, pKS, 1e-10, None, OP.add))
                        pvV.wait(v)
                        V(v, v.reciprocal(rnorm[:, sw * NKVS:(sw + 1) * NKVS], scrK))
                    blk.vector(f_kvev)
                    kvps_free = pvV.n

        PH2 = ExitStack()
        Sp2 = lambda n_, sh, dty: PH2.enter_context(nc.sbuf_tensor(n_, sh, dty)).ap()
        with PH2:
            atnT = Sp2("atnT", [128, NH * TO], BF16)
            with ExitStack() as PH:
                Sp = lambda n_, sh, dty: PH.enter_context(nc.sbuf_tensor(n_, sh, dty)).ap()
                wqbf = [Sp(f"wqbf{i}", [128, DT * 128], BF16) for i in range(2)]
                qbf = Sp("qbf", [128, TO], BF16)
                qs1 = Sp("qs1", [128, TO], F32)
                qs2 = Sp("qs2", [128, TO], F32)
                qs3 = Sp("qs3", [128, TO], F32)
                pQ = [PH.enter_context(nc.psum_tensor(f"pQ{i}", [128, TO], F32)).ap() for i in range(2)]
                pA = [PH.enter_context(nc.psum_tensor(f"pA{i}", [128, TO], F32)).ap() for i in range(2)]

                with nc.Block("p2") as blk:
                    def f_b(g):
                        SBQ.inc(g.dma_start(out=bqc, in_=bqD), 16)
                    blk.gpsimd(f_b)

                    qgrp = [0] * NH
                    agrp = [0] * NH
                    pQfree = [(0, 0)] * 2
                    pAfree = [0] * 2
                    qbf_free = 0
                    qs2_free = 0
                    qs1_prev = 0

                    for h in range(NH):
                        par = h % 2

                        def f_w(g, h=h, par=par, free=(qgrp[h - 2] if h >= 2 else 0)):
                            gT.wait(g, free)
                            SQ[par].inc(g.dma_start(out=wqbf[par], in_=wq[h].rearrange("p t f -> p (t f)")), 16)
                        blk.gpsimd(f_w)
                        w_rdy = SQ[par].n

                        def f_q(t, h=h, par=par, w_rdy=w_rdy, slot=pQfree[par]):
                            SQ[par].wait(t, w_rdy)
                            SBQ.wait(t, 16)
                            t.wait_ge(pvA.h, slot[0])
                            t.wait_ge(pvV.h, slot[1])
                            for d in range(DT):
                                lhs = wqbf[par][:, d * 128:(d + 1) * 128]
                                for qc in range(NQT):
                                    last = t.matmul(pQ[par][:, qc * 512:(qc + 1) * 512], lhs,
                                                    xnt[:, d * TO + qc * 512: d * TO + (qc + 1) * 512],
                                                    start=(d == 0), stop=(d == DT - 1))
                            gT.inc(last)
                        blk.tensor(f_q)
                        qgrp[h] = gT.n

                        def f_phi_v1(v, h=h, par=par, qg=qgrp[h], ps1=qs1_prev):
                            gT.wait(v, qg)
                            SBQ.wait(v, 16)
                            pvA.wait(v, ps1)
                            pvV.wait(v)
                            V(v, v.tensor_scalar(qs3, pQ[par], bqc[:, h:h + 1], 1.0,
                                                 OP.add, OP.add))
                        blk.vector(f_phi_v1)
                        c_s3 = pvV.n

                        def f_phi_a(s, cs3=c_s3, q2f=qs2_free):
                            pvV.wait(s, cs3)
                            pvV.wait(s, q2f)
                            A(s, s.activation(qs1, qs3, AF.Relu, scale=-1.0, bias=1.0))
                            pvA.wait(s)
                            A(s, s.activation(qs2, qs1, AF.Exp, scale=-1.0))
                        blk.scalar(f_phi_a)
                        c_phi_a = pvA.n
                        qs1_prev = pvA.n - 1

                        def f_phi_v(v, h=h, par=par, ca=c_phi_a, qbff=qbf_free):
                            pvA.wait(v, ca)
                            gT.wait(v, qbff)
                            pvV.wait(v)
                            V(v, v.tensor_tensor(qbf, qs2, qs3, OP.max))
                        blk.vector(f_phi_v)
                        c_q = pvV.n
                        qs2_free = pvV.n
                        pQfree[par] = (c_phi_a, c_q)

                        def f_att(t, h=h, par=par, cq=c_q, paf=pAfree[par]):
                            pvV.wait(t, max(cq, paf))
                            g = h // GQ
                            for qc in range(NQT):
                                last = t.matmul(pA[par][:, qc * 512:(qc + 1) * 512],
                                                kvt[:, g * 128:(g + 1) * 128],
                                                qbf[:, qc * 512:(qc + 1) * 512],
                                                start=True, stop=True)
                            gT.inc(last)
                        blk.tensor(f_att)
                        agrp[h] = gT.n
                        qbf_free = gT.n

                        def f_nrm(v, h=h, par=par, ag=agrp[h]):
                            gT.wait(v, ag)
                            V(v, v.tensor_scalar(atnT[:, h * TO:(h + 1) * TO], pA[par],
                                                 rnorm[:, (h // GQ):(h // GQ) + 1], None, OP.mult))
                        blk.vector(f_nrm)
                        pAfree[par] = pvV.n

            with ExitStack() as PH:
                Sp = lambda n_, sh, dty: PH.enter_context(nc.sbuf_tensor(n_, sh, dty)).ap()
                wobf = [Sp(f"wobf{i}", [128, DT * OCH], BF16) for i in range(2)]
                ostg = Sp("ostg", [128, NTO * OCH], BF16)
                bor = Sp("bor", [1, D], BF16)
                pO = [PH.enter_context(nc.psum_tensor(f"pO{i}", [128, OCH], F32)).ap() for i in range(4)]

                with nc.Block("p3") as blk:
                    def f_b(g):
                        SBO.inc(g.dma_start(out=bor, in_=boD), 16)
                    blk.gpsimd(f_b)

                    ogrp = {}
                    pOfree = [0] * 4
                    store = [0] * NOCH

                    for ch in range(NOCH):
                        par = ch % 2

                        PCo = DT // 4
                        if ch == 0:
                            def f_w(g):
                                for p in range(4):
                                    SOP[p].inc(g.dma_start(
                                        out=wobf[0][:, p * PCo * OCH:(p + 1) * PCo * OCH],
                                        in_=wo[0][:, p * PCo:(p + 1) * PCo, :].rearrange("p t f -> p (t f)")), 16)
                            blk.gpsimd(f_w)
                            w_rdy = [SOP[p].n for p in range(4)]
                        else:
                            def f_w(g, ch=ch, par=par,
                                    free=(ogrp[(ch - 2, NTO - 1)] if ch >= 2 else 0)):
                                gT.wait(g, free)
                                SO[par].inc(g.dma_start(out=wobf[par], in_=wo[ch].rearrange("p t f -> p (t f)")), 16)
                            blk.gpsimd(f_w)
                            w_rdy = SO[par].n

                        for tau in range(NTO):
                            k4 = (ch * NTO + tau) % 4

                            def f_o(t, ch=ch, tau=tau, par=par, k4=k4, w_rdy=w_rdy,
                                    pf=pOfree[k4], PCo=PCo):
                                if ch > 0:
                                    SO[par].wait(t, w_rdy)
                                SBO.wait(t, 16)
                                t.wait_ge(pvA.h, pf)
                                for i in range(DT):
                                    if ch == 0 and i % PCo == 0:
                                        SOP[i // PCo].wait(t, w_rdy[i // PCo])
                                    t.matmul(pO[k4], atnT[:, i * TO + tau * 128: i * TO + (tau + 1) * 128],
                                             wobf[par][:, i * OCH:(i + 1) * OCH],
                                             start=(i == 0), stop=False)
                                gT.inc(t.matmul(pO[k4], ones_bf[0:1, 0:128],
                                                bor[0:1, ch * OCH:(ch + 1) * OCH],
                                                start=False, stop=True))
                            blk.tensor(f_o)
                            ogrp[(ch, tau)] = gT.n

                            def f_ev(s, ch=ch, tau=tau, k4=k4, og=ogrp[(ch, tau)],
                                     stf=(store[ch - 1] if ch >= 1 else 0)):
                                gT.wait(s, og)
                                if tau == 0:
                                    SP3.wait(s, stf)
                                A(s, s.copy(ostg[:, tau * OCH:(tau + 1) * OCH], pO[k4]))
                            blk.scalar(f_ev)
                            pOfree[k4] = pvA.n

                        def f_st(s, ch=ch, ev=pvA.n):
                            pvA.wait(s, ev)
                            SP3.inc(s.dma_start(out=ascr[ch].rearrange("p t f -> p (t f)"), in_=ostg), 16)
                        blk.scalar(f_st)
                        store[ch] = SP3.n

        with ExitStack() as PH:
            Sp = lambda n_, sh, dty: PH.enter_context(nc.sbuf_tensor(n_, sh, dty)).ap()
            webf = [Sp(f"webf{i}", [128, DT * HCH], BF16) for i in range(2)]
            bb_sb = [Sp(f"bb_sb{i}", [128, HCH], BF16) for i in range(2)]
            acc = Sp("acc", [128, NTO * HCH], F32)
            atns = Sp("atns", [128, NTO * HCH], BF16)
            ostg5 = [Sp(f"ostg5{i}", [128, NTO * HCH], F32) for i in range(2)]
            trl = [Sp(f"trl{i}", [128, HCH], F32) for i in range(2)]
            ber = Sp("ber", [1, E * HCH], BF16)
            pE = [PH.enter_context(nc.psum_tensor(f"pE{i}", [128, HCH], F32)).ap() for i in range(4)]
            pBB = PH.enter_context(nc.psum_tensor("pBB", [128, HCH], F32)).ap()
            HOC = HCH // OCH

            with nc.Block("p5") as blk:
                p3_done = SP3.n

                def f_pro_s(s):
                    SP3.wait(s, p3_done)
                blk.sync(f_pro_s)

                def f_pro_a(s):
                    SP3.wait(s, p3_done)
                blk.scalar(f_pro_a)

                def f_pro_g(g):
                    SP3.wait(g, p3_done)
                blk.gpsimd(f_pro_g)

                egrp = {}
                bbev = {}
                bb_free = [0] * 2
                pEfree = [0] * 4
                trl_free = [0] * 2
                outst = [0] * NCH
                combine_done = 0
                ber_free = 0

                for ch in range(NCH):
                    spar = ch % 2

                    def f_x(s, ch=ch, spar=spar, of=(outst[ch - 2] if ch >= 2 else 0)):
                        SOUT[spar].wait(s, of)
                        SXF[spar].inc(s.dma_start(out=ostg5[spar], in_=xoc[ch].rearrange("p t f -> p (t f)")), 16)
                    blk.sync(f_x)
                    xfill = SXF[spar].n

                    def f_at(s, ch=ch, cd=combine_done):
                        pvV.wait(s, cd)
                        for oc in range(HOC):
                            SAT.inc(s.dma_start(
                                out=atns.rearrange("p (t f) -> p t f", t=NTO)[:, :, oc * OCH:(oc + 1) * OCH],
                                in_=ascr[ch * HOC + oc]), 16)
                    blk.scalar(f_at)
                    atld = SAT.n

                    def f_be(g, ch=ch, bf=ber_free):
                        gT.wait(g, bf)
                        SBE.inc(g.dma_start(out=ber, in_=beD[ch:ch + 1, :]), 16)
                    blk.gpsimd(f_be)
                    ber_rdy = SBE.n

                    for e in range(E):
                        wpar = e % 2
                        k2 = ch * E + e

                        PCe = DT // 4
                        if k2 == 0:
                            def f_w(g):
                                for p in range(4):
                                    SEP[p].inc(g.dma_start(
                                        out=webf[0][:, p * PCe * HCH:(p + 1) * PCe * HCH],
                                        in_=we[0, 0][:, p * PCe:(p + 1) * PCe, :].rearrange("p t f -> p (t f)")), 16)
                            blk.gpsimd(f_w)
                            w_rdy = [SEP[p].n for p in range(4)]
                        else:
                            def f_w(g, ch=ch, e=e, wpar=wpar,
                                    free=(egrp[(ch * E + e - 2, NTO - 1)] if k2 >= 2 else 0)):
                                gT.wait(g, free)
                                SE[wpar].inc(g.dma_start(out=webf[wpar], in_=we[ch, e].rearrange("p t f -> p (t f)")), 16)
                            blk.gpsimd(f_w)
                            w_rdy = SE[wpar].n

                        def f_bb(t, ch=ch, e=e, ber_rdy=ber_rdy, bbe=(bbev.get((ch * E + e - 1,), 0))):
                            SBE.wait(t, ber_rdy)
                            t.wait_ge(pvA.h, bbe)
                            gT.inc(t.matmul(pBB, ones_bf[0:1, 0:128],
                                            ber[0:1, e * HCH:(e + 1) * HCH], start=True, stop=True))
                        blk.tensor(f_bb)
                        c_bb = gT.n
                        if e == E - 1:
                            ber_free_new = c_bb

                        def f_bbe(s, ch=ch, e=e, wpar=wpar, cb=c_bb, bf=bb_free[wpar]):
                            gT.wait(s, cb)
                            pvV.wait(s, bf)
                            A(s, s.copy(bb_sb[wpar], pBB))
                        blk.scalar(f_bbe)
                        bbev[(ch * E + e,)] = pvA.n
                        bb_rdy = pvA.n

                        for tau in range(NTO):
                            k4 = (k2 * NTO + tau) % 4

                            def f_e(t, ch=ch, e=e, tau=tau, wpar=wpar, k4=k4, k2=k2,
                                    w_rdy=w_rdy, pf=pEfree[k4], PCe=PCe):
                                if k2 > 0:
                                    SE[wpar].wait(t, w_rdy)
                                t.wait_ge(pvV.h, pf)
                                for d in range(DT):
                                    if k2 == 0 and d % PCe == 0:
                                        SEP[d // PCe].wait(t, w_rdy[d // PCe])
                                    last = t.matmul(pE[k4],
                                                    xnt[:, d * TO + tau * 128: d * TO + (tau + 1) * 128],
                                                    webf[wpar][:, d * HCH:(d + 1) * HCH],
                                                    start=(d == 0), stop=(d == DT - 1))
                                gT.inc(last)
                            blk.tensor(f_e)
                            egrp[(k2, tau)] = gT.n

                            def f_r(v, ch=ch, e=e, tau=tau, k4=k4, wpar=wpar, eg=egrp[(k2, tau)],
                                    cd=combine_done, tf=trl_free[tau % 2], br=bb_rdy):
                                gT.wait(v, eg)
                                pvA.wait(v, br)
                                if e == 0 and tau == 0:
                                    pvV.wait(v, max(cd, mask_done))
                                pvV.wait(v, tf)
                                V(v, v.tensor_tensor(trl[tau % 2], pE[k4], bb_sb[wpar], OP.add))
                                pvV.wait(v)
                                if e == 0:
                                    V(v, v.tensor_scalar(acc[:, tau * HCH:(tau + 1) * HCH],
                                                         trl[tau % 2],
                                                         mask[:, tau * E:tau * E + 1], 0.0,
                                                         OP.mult, OP.max))
                                else:
                                    V(v, v.tensor_scalar(trl[tau % 2], trl[tau % 2],
                                                         mask[:, tau * E + e:tau * E + e + 1], 0.0,
                                                         OP.mult, OP.max))
                                    pvV.wait(v)
                                    V(v, v.tensor_tensor(acc[:, tau * HCH:(tau + 1) * HCH],
                                                         trl[tau % 2],
                                                         acc[:, tau * HCH:(tau + 1) * HCH], OP.add))
                            blk.vector(f_r)
                            pEfree[k4] = pvV.n
                            trl_free[tau % 2] = pvV.n
                            if tau == NTO - 1:
                                bb_free[wpar] = pvV.n

                            if e == E - 1:
                                def f_cmb(v, ch=ch, tau=tau, spar=spar, xfill=xfill, atld=atld):
                                    SXF[spar].wait(v, xfill)
                                    SAT.wait(v, atld)
                                    sl = slice(tau * HCH, (tau + 1) * HCH)
                                    pvV.wait(v)
                                    V(v, v.tensor_tensor(trl[tau % 2], acc[:, sl], atns[:, sl], OP.add))
                                    pvV.wait(v)
                                    V(v, v.tensor_tensor(ostg5[spar][:, sl], trl[tau % 2],
                                                         ostg5[spar][:, sl], OP.add))
                                blk.vector(f_cmb)
                                trl_free[tau % 2] = pvV.n
                    ber_free = ber_free_new
                    combine_done = pvV.n

                    def f_out(s, ch=ch, spar=spar, cd=combine_done):
                        pvV.wait(s, cd)
                        SOUT[spar].inc(s.dma_start(
                            out=out.rearrange("(t p) n -> p t n", p=128)[:, :, ch * HCH:(ch + 1) * HCH],
                            in_=ostg5[spar].rearrange("p (t f) -> p t f", t=NTO)), 16)
                    blk.sync(f_out)
                    outst[ch] = SOUT[spar].n

                def f_fin(s):
                    for chn in [SOUT[0], SOUT[1], SXF[0], SXF[1], SXA[0], SXA[1],
                                SXB[0], SXB[1], SAT, SP3,
                                SPRE, SKV, SQ[0], SQ[1], SBQ, SO[0], SO[1], SBO,
                                SE[0], SE[1], SBE,
                                STR[0], STR[1],
                                SKP[0], SKP[1], SKP[2], SKP[3], SVP[0], SVP[1], SVP[2], SVP[3],
                                SOP[0], SOP[1], SOP[2], SOP[3], SEP[0], SEP[1], SEP[2], SEP[3]]:
                        chn.wait(s)
                blk.sync(f_fin)

    return nc


def kernel(**inputs):
    c = mk_cfg()
    D, TO, DT, NTO = c["D"], c["TO"], c["DT"], c["NTO"]
    NCH, HCH, NOCH, OCH, KSW, NH, E, KD = (c["NCH"], c["HCH"], c["NOCH"], c["OCH"],
                                           c["KSW"], c["NH"], c["E"], c["KD"])
    f32 = np.float32
    Wq = np.asarray(inputs["L0_attn_Wq"], f32)
    Wk = np.asarray(inputs["L0_attn_Wk"], f32)
    Wv = np.asarray(inputs["L0_attn_Wv"], f32)
    Wo = np.asarray(inputs["L0_attn_Wo"], f32)
    We = np.asarray(inputs["L0_ffn_W_e"], f32)
    Wg = np.asarray(inputs["L0_ffn_W_gate"], f32)

    wq_h = np.ascontiguousarray(Wq.T.reshape(DT, 128, NH, 128).transpose(2, 1, 0, 3))
    wk_h = np.ascontiguousarray(Wk.T.reshape(DT, 128, 2, KSW).transpose(2, 1, 0, 3))
    wv_h = np.ascontiguousarray(Wv.T.reshape(DT, 128, 2, KSW).transpose(2, 1, 0, 3))
    wo_h = np.ascontiguousarray(Wo.T.reshape(DT, 128, NOCH, OCH).transpose(2, 1, 0, 3))
    we_h = np.ascontiguousarray(We.reshape(E, DT, 128, NCH, HCH).transpose(3, 0, 2, 1, 4))
    wg_h = np.ascontiguousarray(Wg.reshape(DT, 128, E).transpose(1, 0, 2))
    bq_h = np.ascontiguousarray(np.asarray(inputs["L0_attn_bq"], f32).reshape(NH, 128).T)
    bk_h = np.asarray(inputs["L0_attn_bk"], f32).reshape(1, KD)
    bv_h = np.asarray(inputs["L0_attn_bv"], f32).reshape(1, KD)
    bo_h = np.asarray(inputs["L0_attn_bo"], f32).reshape(1, D)
    bg_h = np.asarray(inputs["L0_ffn_b_gate"], f32).reshape(E, 1)
    be_h = np.ascontiguousarray(
        np.asarray(inputs["L0_ffn_b_e"], f32).reshape(E, NCH, HCH).transpose(1, 0, 2)
    ).reshape(NCH, E * HCH)

    x = np.asarray(inputs["L0_attn_hidden_states"], f32)
    nc = build_program(c)

    in_maps = []
    for core in range(8):
        b, half = core // 2, core % 2
        xown = np.ascontiguousarray(x[b, half * TO:(half + 1) * TO, :])
        xoth = np.ascontiguousarray(x[b, (1 - half) * TO:(2 - half) * TO, :])
        xoc = np.ascontiguousarray(xown.reshape(NTO, 128, NCH, HCH).transpose(2, 1, 0, 3))
        in_maps.append({
            "xo": xown, "xoc": xoc, "xb": xoth,
            "wq": wq_h, "wk": wk_h, "wv": wv_h, "wo": wo_h, "we": we_h, "wg": wg_h,
            "bq": bq_h, "bk": bk_h, "bv": bv_h, "bo": bo_h, "bg": bg_h, "be": be_h,
        })

    res = run_bass_kernel_spmd(nc, in_maps, core_ids=list(range(8)))
    full = np.empty((4, 2048, 4096), dtype=np.float32)
    for core in range(8):
        b, half = core // 2, core % 2
        full[b, half * TO:(half + 1) * TO, :] = res.results[core]["out"]
    return full


# version 20
# speedup vs baseline: 1.1743x; 1.1743x over previous
import numpy as np
import ml_dtypes
import concourse.bass as bass
import concourse.mybir as mybir
from concourse.alu_op_type import AluOpType as OP
from concourse.bass_utils import run_bass_kernel_spmd
from contextlib import ExitStack

F32 = mybir.dt.float32
BF16 = mybir.dt.bfloat16
AF = mybir.ActivationFunctionType


def mk_cfg(D=4096, TO=1024, NH=32, NKV=8, E=8, HCH=512, OCH=256):
    c = {}
    c["D"] = D; c["DT"] = D // 128
    c["TO"] = TO; c["NTO"] = TO // 128
    c["TB"] = 2 * TO; c["NTB"] = 2 * TO // 128
    c["NH"] = NH; c["NKV"] = NKV; c["GQ"] = NH // NKV
    c["KD"] = NKV * 128
    c["E"] = E
    c["HCH"] = HCH; c["NCH"] = D // HCH
    c["OCH"] = OCH; c["NOCH"] = D // OCH
    c["KSW"] = c["KD"] // 2
    c["NKVS"] = NKV // 2
    c["NQT"] = TO // 512
    assert NH * 128 == D and TO % 512 == 0 and NKV % 2 == 0
    return c


class Chan:
    def __init__(self, nc, name):
        self.h = nc.alloc_semaphore(name)
        self.n = 0

    def inc(self, instr, k=1):
        instr.then_inc(self.h, k)
        self.n += k
        return self.n

    def wait(self, eng, val=None):
        eng.wait_ge(self.h, self.n if val is None else val)


def build_program(c):
    nc = bass.Bass("TRN2", target_bir_lowering=False)
    D, DT, TO, NTO, TB, NTB = c["D"], c["DT"], c["TO"], c["NTO"], c["TB"], c["NTB"]
    KD, NH, NKV, GQ, E = c["KD"], c["NH"], c["NKV"], c["GQ"], c["E"]
    HCH, NCH, OCH, NOCH = c["HCH"], c["NCH"], c["OCH"], c["NOCH"]
    KSW, NKVS, NQT = c["KSW"], c["NKVS"], c["NQT"]

    dt = nc.dram_tensor
    xo = dt("xo", [TO, D], F32, kind="ExternalInput").ap()
    xoc = dt("xoc", [NCH, 128, NTO, HCH], F32, kind="ExternalInput").ap()
    xb = dt("xb", [TO, D], F32, kind="ExternalInput").ap()
    wq = dt("wq", [NH, 128, DT, 128], F32, kind="ExternalInput").ap()
    wk = dt("wk", [2, 128, DT, KSW], F32, kind="ExternalInput").ap()
    wv = dt("wv", [2, 128, DT, KSW], F32, kind="ExternalInput").ap()
    wo = dt("wo", [NOCH, 128, DT, OCH], F32, kind="ExternalInput").ap()
    we = dt("we", [NCH, E, 128, DT, HCH], F32, kind="ExternalInput").ap()
    wg = dt("wg", [128, DT, E], F32, kind="ExternalInput").ap()
    bqD = dt("bq", [128, NH], F32, kind="ExternalInput").ap()
    bkD = dt("bk", [1, KD], F32, kind="ExternalInput").ap()
    bvD = dt("bv", [1, KD], F32, kind="ExternalInput").ap()
    boD = dt("bo", [1, D], F32, kind="ExternalInput").ap()
    bgD = dt("bg", [E, 1], F32, kind="ExternalInput").ap()
    beD = dt("be", [NCH, E * HCH], F32, kind="ExternalInput").ap()
    out = dt("out", [TO, D], F32, kind="ExternalOutput").ap()
    ascr = dt("ascr", [NOCH, 128, NTO, OCH], BF16, kind="Internal").ap()

    id_bf_h = nc.inline_tensor(np.eye(128, dtype=ml_dtypes.bfloat16), name="id_bf_h")
    id_f_h = nc.inline_tensor(np.eye(E, dtype=np.float32), name="id_f_h")

    PERS = ExitStack()
    S = lambda name, shape, dty: PERS.enter_context(nc.sbuf_tensor(name, shape, dty)).ap()
    with PERS:
        xnt = S("xnt", [128, DT * TO], BF16)
        kvt = S("kvt", [128, NKV * 128], BF16)
        rnorm = S("rnorm", [128, NKV], F32)
        logits = S("logits", [128, NTO * E], F32)
        mask = S("mask", [128, NTO * E], F32)
        ones_bf = S("ones_bf", [128, 128], BF16)
        idbf = S("idbf", [128, 128], BF16)
        idf = S("idf", [E, E], F32)
        wg_f = S("wg_f", [128, DT * E], F32)
        wg_hi = S("wg_hi", [128, DT * E], BF16)
        wg_lo = S("wg_lo", [128, DT * E], BF16)
        bg_c = S("bg_c", [E, 1], F32)
        bqc = S("bqc", [128, NH], F32)
        bkb = S("bkb", [128, KD], BF16)
        bvb = S("bvb", [128, KD], BF16)
        st6 = S("st6", [128, 6 * (D // 512)], F32)
        st2 = S("st2", [128, 2], F32)
        mv = S("mv", [128, 4], F32)
        t8 = S("t8", [128, 8], F32)

        SXA = [Chan(nc, "SXA0"), Chan(nc, "SXA1")]
        SXB = [Chan(nc, "SXB0"), Chan(nc, "SXB1")]
        SXF = [Chan(nc, "SXF0"), Chan(nc, "SXF1")]
        SOUT = [Chan(nc, "SOUT0"), Chan(nc, "SOUT1")]
        SPRE = Chan(nc, "SPRE")
        SKV = Chan(nc, "SKV")
        SQ = [Chan(nc, "SQ0"), Chan(nc, "SQ1")]
        SBQ = Chan(nc, "SBQ")
        SO = [Chan(nc, "SO0"), Chan(nc, "SO1")]
        SBO = Chan(nc, "SBO")
        SE = [Chan(nc, "SE0"), Chan(nc, "SE1")]
        SBE = Chan(nc, "SBE")
        STR = [[Chan(nc, "STR00"), Chan(nc, "STR01")],
               [Chan(nc, "STR10"), Chan(nc, "STR11")]]
        SKP = [Chan(nc, f"SKP{i}") for i in range(4)]
        SVP = [Chan(nc, f"SVP{i}") for i in range(4)]
        SOP = [Chan(nc, f"SOP{i}") for i in range(4)]
        SEP = [Chan(nc, f"SEP{i}") for i in range(4)]
        SP3 = Chan(nc, "SP3")
        SAT = Chan(nc, "SAT")
        pvV = Chan(nc, "pvV")
        pvA = Chan(nc, "pvA")
        gT = Chan(nc, "gT")

        def V(v, instr):
            return pvV.inc(instr)

        def A(s, instr):
            return pvA.inc(instr)

        with ExitStack() as PH:
            Sp = lambda n_, sh, dty: PH.enter_context(nc.sbuf_tensor(n_, sh, dty)).ap()
            xbufs = [Sp("xbufA", [128, D], F32), Sp("xbufB", [128, D], F32)]
            xnfs = [Sp("xnfA", [128, D], F32), Sp("xnfB", [128, D], F32)]
            xhis = [Sp("xhiA", [128, D], BF16), Sp("xhiB", [128, D], BF16)]
            xlos = [Sp("xloA", [128, D], BF16), Sp("xloB", [128, D], BF16)]
            lots = [Sp("lotA", [128, D], BF16), Sp("lotB", [128, D], BF16)]
            glog_all = Sp("glog_all", [E, NTO * 128], F32)
            ptH = [PH.enter_context(nc.psum_tensor(f"ptH{i}", [128, 128], BF16)).ap() for i in range(2)]
            ptL = [PH.enter_context(nc.psum_tensor(f"ptL{i}", [128, 128], BF16)).ap() for i in range(2)]
            pG = [PH.enter_context(nc.psum_tensor(f"pG{i}", [E, 128], F32)).ap() for i in range(2)]
            pGT = [PH.enter_context(nc.psum_tensor(f"pGT{i}", [128, E], F32)).ap() for i in range(2)]

            with nc.Block("a0") as blk:
                def prelude_g(g):
                    SPRE.inc(g.dma_start(out=idbf, in_=id_bf_h.ap()), 16)
                    SPRE.inc(g.dma_start(out=idf, in_=id_f_h.ap()), 16)
                    SPRE.inc(g.dma_start(out=wg_f, in_=wg.rearrange("p t e -> p (t e)")), 16)
                    SPRE.inc(g.dma_start(out=bg_c, in_=bgD), 16)
                blk.gpsimd(prelude_g)
                consts_rdy = SPRE.n

                def prelude_v(v):
                    SPRE.wait(v, consts_rdy)
                    V(v, v.memset(ones_bf, 1.0))
                    pvV.wait(v)
                    V(v, v.tensor_copy(wg_hi, wg_f))
                    pvV.wait(v)
                    V(v, v.tensor_tensor(wg_lo, wg_f, wg_hi, OP.subtract))
                blk.vector(prelude_v)
                wg_rdy = pvV.n

                xbuf_free = [0, 0]
                xn_free = [0, 0]
                gate_done = [0] * NTO
                pG_free = 0
                NB = D // 512

                for tau in range(NTO):
                    xbuf = xbufs[tau % 2]
                    xnf, xhi, xlo, lot = xnfs[tau % 2], xhis[tau % 2], xlos[tau % 2], lots[tau % 2]
                    def f_sx(s, tau=tau, xf=xbuf_free[tau % 2], xbuf=xbuf):
                        pvV.wait(s, xf)
                        SXA[tau % 2].inc(s.dma_start(out=xbuf, in_=xo[tau * 128:(tau + 1) * 128, :]), 16)
                    blk.sync(f_sx)
                    x_rdy = SXA[tau % 2].n

                    def f_ln1(v, x_rdy=x_rdy, xbuf=xbuf, tau=tau):
                        SXA[tau % 2].wait(v, x_rdy)
                        for i in range(NB):
                            pvV.wait(v)
                            V(v, v.bn_stats(st6[:, 6 * i:6 * (i + 1)], xbuf[:, 512 * i:512 * (i + 1)]))
                        pvV.wait(v)
                        V(v, v.bn_aggr(st2, st6))
                        pvV.wait(v)
                        V(v, v.tensor_scalar(mv[:, 2:3], st2[:, 1:2], 1e-5, None, OP.add))
                    blk.vector(f_ln1)
                    c_ln1 = pvV.n

                    def f_sq(s, c=c_ln1):
                        pvV.wait(s, c)
                        A(s, s.activation(mv[:, 3:4], mv[:, 2:3], AF.Ln))
                        pvA.wait(s)
                        A(s, s.activation(mv[:, 1:2], mv[:, 3:4], AF.Exp, scale=-0.5))
                    blk.scalar(f_sq)
                    c_sqrt = pvA.n

                    def f_ln2(v, c=c_sqrt, xnfree=xn_free[tau % 2], xbuf=xbuf,
                              xnf=xnf, xhi=xhi, xlo=xlo):
                        pvA.wait(v, c)
                        gT.wait(v, xnfree)
                        V(v, v.tensor_scalar(xnf, xbuf, st2[:, 0:1], mv[:, 1:2], OP.subtract, OP.mult))
                        pvV.wait(v)
                        V(v, v.tensor_copy(xhi, xnf))
                        pvV.wait(v)
                        V(v, v.tensor_tensor(xlo, xnf, xhi, OP.subtract))
                    blk.vector(f_ln2)
                    xbuf_free[tau % 2] = pvV.n
                    c_xn = pvV.n

                    baseA = pvA.n
                    th = [0] * DT
                    tl = [0] * DT

                    def f_tr(t, tau=tau, c_xn=c_xn, baseA=baseA, xhi=xhi, xlo=xlo):
                        pvV.wait(t, c_xn)
                        for j in range(DT):
                            if j >= 2:
                                t.wait_ge(pvA.h, baseA + 2 * j - 3)
                            th[j] = gT.inc(t.transpose(ptH[j % 2], xhi[:, j * 128:(j + 1) * 128], idbf))
                            if j >= 2:
                                t.wait_ge(pvA.h, baseA + 2 * j - 2)
                            tl[j] = gT.inc(t.transpose(ptL[j % 2], xlo[:, j * 128:(j + 1) * 128], idbf))
                    blk.tensor(f_tr)
                    xn_free[tau % 2] = gT.n

                    def f_ev(s, tau=tau, lot=lot):
                        for j in range(DT):
                            gT.wait(s, th[j])
                            A(s, s.copy(xnt[:, j * TO + tau * 128: j * TO + (tau + 1) * 128], ptH[j % 2]))
                            gT.wait(s, tl[j])
                            A(s, s.copy(lot[:, j * 128:(j + 1) * 128], ptL[j % 2]))
                    blk.scalar(f_ev)
                    assert pvA.n == baseA + 2 * DT
                    ev_done_A = pvA.n
                    ev_done_V = pvV.n

                    def f_gate(t, tau=tau, a=ev_done_A, vv=ev_done_V, pf=pG_free, lot=lot):
                        pvA.wait(t, max(a, pf))
                        pvV.wait(t, max(vv, wg_rdy))
                        SPRE.wait(t, consts_rdy)
                        for d in range(DT):
                            xs = xnt[:, d * TO + tau * 128: d * TO + (tau + 1) * 128]
                            ls = lot[:, d * 128:(d + 1) * 128]
                            pg = pG[tau % 2]
                            t.matmul(pg, wg_hi[:, d * E:(d + 1) * E], xs, start=(d == 0), stop=False)
                            t.matmul(pg, wg_lo[:, d * E:(d + 1) * E], xs, start=False, stop=False)
                            last = t.matmul(pg, wg_hi[:, d * E:(d + 1) * E], ls, start=False,
                                            stop=(d == DT - 1))
                        gT.inc(last)
                    blk.tensor(f_gate)
                    gate_done[tau] = gT.n

                    def emit_gcopy(tau2):
                        def f_gc(s, tau2=tau2, c=gate_done[tau2]):
                            gT.wait(s, c)
                            A(s, s.copy(glog_all[:, tau2 * 128:(tau2 + 1) * 128], pG[tau2 % 2]))
                        blk.scalar(f_gc)
                        return pvA.n
                    if tau >= 1:
                        gc = emit_gcopy(tau - 1)
                        if tau >= 1:
                            pG_free_next = gc
                    if tau >= 1:
                        pG_free = pG_free_prev if tau >= 2 else 0
                        pG_free_prev = pG_free_next
                pG_free = emit_gcopy(NTO - 1)
                gcopies = pvA.n

                def f_bg(v, c=gcopies):
                    pvA.wait(v, c)
                    V(v, v.tensor_scalar(glog_all, glog_all, bg_c, None, OP.add))
                blk.vector(f_bg)
                c_bg = pvV.n

                gtr = [0] * NTO
                def f_gtr_all(t, c=c_bg):
                    pvV.wait(t, c)
                    for tau in range(NTO):
                        if tau >= 2:
                            t.wait_ge(pvV.h, c + 3 * (tau - 1))
                        gtr[tau] = gT.inc(t.transpose(pGT[tau % 2],
                                                      glog_all[:, tau * 128:(tau + 1) * 128],
                                                      idf))
                blk.tensor(f_gtr_all)

                def f_msk_all(v, c=c_bg):
                    for tau in range(NTO):
                        gT.wait(v, gtr[tau])
                        V(v, v.tensor_copy(logits[:, tau * E:(tau + 1) * E], pGT[tau % 2]))
                        pvV.wait(v)
                        V(v, v.max(out=t8, in_=logits[:, tau * E:(tau + 1) * E]))
                        pvV.wait(v)
                        V(v, v.tensor_scalar(mask[:, tau * E:(tau + 1) * E],
                                             logits[:, tau * E:(tau + 1) * E],
                                             t8[:, 1:2], 0.5, OP.is_ge, OP.mult))
                    assert pvV.n == c + 3 * NTO
                blk.vector(f_msk_all)
            mask_done = pvV.n

        with ExitStack() as PH:
            Sp = lambda n_, sh, dty: PH.enter_context(nc.sbuf_tensor(n_, sh, dty)).ap()
            wkbf = Sp("wkbf", [128, DT * KSW], BF16)
            wvbf = Sp("wvbf", [128, DT * KSW], BF16)
            xbuf1s = [Sp("xbuf1A", [128, D], F32), Sp("xbuf1B", [128, D], F32)]
            xhi1s = [Sp("xhi1A", [128, D], BF16), Sp("xhi1B", [128, D], BF16)]
            xntt = Sp("xntt", [128, D], BF16)
            ktoks = [Sp("ktokA", [128, KSW], BF16), Sp("ktokB", [128, KSW], BF16)]
            vtoks = [Sp("vtokA", [128, KSW], BF16), Sp("vtokB", [128, KSW], BF16)]
            scr1 = Sp("scr1", [128, KSW], F32)
            scr3 = Sp("scr3", [128, KSW], F32)
            scrK = Sp("scrK", [128, NKVS], F32)
            bkr = Sp("bkr", [1, KD], BF16)
            bvr = Sp("bvr", [1, KD], BF16)
            pK = [PH.enter_context(nc.psum_tensor(f"pK{i}", [128, KSW], F32)).ap() for i in range(2)]
            pV = [PH.enter_context(nc.psum_tensor(f"pV{i}", [128, KSW], F32)).ap() for i in range(2)]
            pKV = PH.enter_context(nc.psum_tensor("pKV", [128, NKVS * 128], F32)).ap()
            pKS = PH.enter_context(nc.psum_tensor("pKS", [128, NKVS], F32)).ap()
            ptC = [PH.enter_context(nc.psum_tensor(f"ptC{i}", [128, 128], BF16)).ap() for i in range(2)]

            with nc.Block("a1") as blk:
                def f_b(g):
                    SKV.inc(g.dma_start(out=bkr, in_=bkD), 16)
                    SKV.inc(g.dma_start(out=bvr, in_=bvD), 16)
                blk.gpsimd(f_b)

                CWb = min(512, KSW)
                bb_chunks = [(off, min(CWb, KD - off)) for off in range(0, KD, CWb)]

                def f_bb(t):
                    SKV.wait(t, 32)
                    SPRE.wait(t, consts_rdy)
                    for i, (off, w) in enumerate(bb_chunks):
                        t.matmul(pK[i % 2][:, 0:w], ones_bf[0:1, 0:128],
                                 bkr[0:1, off:off + w], start=True, stop=True)
                        gT.inc(t.matmul(pV[i % 2][:, 0:w], ones_bf[0:1, 0:128],
                                        bvr[0:1, off:off + w], start=True, stop=True))
                blk.tensor(f_bb)
                c_bbmm = gT.n

                def f_bbe(s):
                    gT.wait(s, c_bbmm)
                    for i, (off, w) in enumerate(bb_chunks):
                        pvA.wait(s)
                        A(s, s.copy(bkb[:, off:off + w], pK[i % 2][:, 0:w]))
                        pvA.wait(s)
                        A(s, s.copy(bvb[:, off:off + w], pV[i % 2][:, 0:w]))
                blk.scalar(f_bbe)
                c_bbe = pvA.n

                kw_free = 0
                kvps_free = 0
                xbuf1_free = [0, 0]
                xhi1_free = [(0, 0, 0), (1, 0, 0)]
                ktok_free = [0, 0]
                c_max_prev = 0
                c_scr1_prev = 0
                pKfree = [(c_bbe, 0)] * 2

                for sw in range(2):
                    PC = DT // 4
                    def f_w(g, sw=sw, kwf=kw_free):
                        gT.wait(g, kwf)
                        for p in range(4):
                            SKP[p].inc(g.dma_start(
                                out=wkbf[:, p * PC * KSW:(p + 1) * PC * KSW],
                                in_=wk[sw][:, p * PC:(p + 1) * PC, :].rearrange("p t f -> p (t f)")), 16)
                            SVP[p].inc(g.dma_start(
                                out=wvbf[:, p * PC * KSW:(p + 1) * PC * KSW],
                                in_=wv[sw][:, p * PC:(p + 1) * PC, :].rearrange("p t f -> p (t f)")), 16)
                    blk.gpsimd(f_w)
                    w_rdy = [(SKP[p].n, SVP[p].n) for p in range(4)]

                    def emit_ln(ot):
                        xbuf1 = xbuf1s[ot % 2]
                        xhi1 = xhi1s[ot % 2]

                        def f_sx(s, ot=ot, xf=xbuf1_free[ot % 2], xbuf1=xbuf1):
                            pvV.wait(s, xf)
                            SXB[ot % 2].inc(s.dma_start(out=xbuf1, in_=xb[ot * 128:(ot + 1) * 128, :]), 16)
                        blk.sync(f_sx)
                        x_rdy = SXB[ot % 2].n

                        def f_ln1(v, x_rdy=x_rdy, xbuf1=xbuf1, ot=ot):
                            SXB[ot % 2].wait(v, x_rdy)
                            for i in range(D // 512):
                                pvV.wait(v)
                                V(v, v.bn_stats(st6[:, 6 * i:6 * (i + 1)], xbuf1[:, 512 * i:512 * (i + 1)]))
                            pvV.wait(v)
                            V(v, v.bn_aggr(st2, st6))
                            pvV.wait(v)
                            V(v, v.tensor_scalar(mv[:, 2:3], st2[:, 1:2], 1e-5, None, OP.add))
                        blk.vector(f_ln1)
                        c_ln1 = pvV.n

                        def f_sq(s, ccc=c_ln1):
                            pvV.wait(s, ccc)
                            A(s, s.activation(mv[:, 3:4], mv[:, 2:3], AF.Ln))
                            pvA.wait(s)
                            A(s, s.activation(mv[:, 1:2], mv[:, 3:4], AF.Exp, scale=-0.5))
                        blk.scalar(f_sq)
                        c_sqrt = pvA.n

                        def f_ln2(v, ccc=c_sqrt, xhf=xhi1_free[ot % 2], xbuf1=xbuf1, xhi1=xhi1):
                            pvA.wait(v, ccc)
                            STR[0][xhf[0]].wait(v, xhf[1])
                            STR[1][xhf[0]].wait(v, xhf[2])
                            V(v, v.tensor_scalar(xhi1, xbuf1, st2[:, 0:1], mv[:, 1:2],
                                                 OP.subtract, OP.mult))
                        blk.vector(f_ln2)
                        xbuf1_free[ot % 2] = pvV.n
                        return pvV.n

                    other_kv_prev = [0]

                    def emit_tr(ot, c_xn):
                        xhi1 = xhi1s[ot % 2]
                        kvp = other_kv_prev[0]

        
                        pp = ot % 2

                        def f_t0(s, c_xn=c_xn, kvp=kvp, xhi1=xhi1, pp=pp):
                            pvV.wait(s, c_xn)
                            gT.wait(s, kvp)
                            for j in range(0, DT, 2):
                                STR[0][pp].inc(s.dma_start_transpose(
                                    xntt[:, j * 128:(j + 1) * 128],
                                    xhi1[:, j * 128:(j + 1) * 128]), 16)
                        blk.sync(f_t0)

                        def f_t1(sc, c_xn=c_xn, kvp=kvp, xhi1=xhi1, pp=pp):
                            pvV.wait(sc, c_xn)
                            gT.wait(sc, kvp)
                            for j in range(1, DT, 2):
                                STR[1][pp].inc(sc.dma_start_transpose(
                                    xntt[:, j * 128:(j + 1) * 128],
                                    xhi1[:, j * 128:(j + 1) * 128]), 16)
                        blk.scalar(f_t1)
                        xhi1_free[ot % 2] = (pp, STR[0][pp].n, STR[1][pp].n)
                        return (pp, STR[0][pp].n, STR[1][pp].n)

                    ln_done = {}
                    for idx, bt in enumerate([x for pr in zip(range(NTO), range(NTO, NTB)) for x in pr]):
                        own = bt < NTO
                        if own and bt < NTO:
                            ln_done[bt] = emit_ln(bt)
                        if not own:
                            xntt_rdy = emit_tr(bt - NTO, ln_done[bt - NTO])
                        else:
                            xntt_rdy = (0, 0, 0)
                        par = idx % 2

                        def f_kv(t, bt=bt, own=own, par=par, sw=sw, w_rdy=w_rdy,
                                 xntt_rdy=xntt_rdy, slot=pKfree[par], PC=PC):
                            if not own:
                                STR[0][xntt_rdy[0]].wait(t, xntt_rdy[1])
                                STR[1][xntt_rdy[0]].wait(t, xntt_rdy[2])
                            t.wait_ge(pvA.h, slot[0])
                            t.wait_ge(pvV.h, slot[1])
                            for d in range(DT):
                                if d % PC == 0:
                                    SKP[d // PC].wait(t, w_rdy[d // PC][0])
                                    SVP[d // PC].wait(t, w_rdy[d // PC][1])
                                if own:
                                    xs = xnt[:, d * TO + bt * 128: d * TO + (bt + 1) * 128]
                                else:
                                    xs = xntt[:, d * 128:(d + 1) * 128]
                                t.matmul(pK[par], xs, wkbf[:, d * KSW:(d + 1) * KSW],
                                         start=(d == 0), stop=(d == DT - 1))
                                last = t.matmul(pV[par], xs, wvbf[:, d * KSW:(d + 1) * KSW],
                                                start=(d == 0), stop=(d == DT - 1))
                            gT.inc(last)
                        blk.tensor(f_kv)
                        c_kv = gT.n
                        if not own:
                            other_kv_prev[0] = gT.n
                        if idx == NTB - 1:
                            kw_free = gT.n

                        ktok = ktoks[idx % 2]
                        vtok = vtoks[idx % 2]

                        def f_phi_v1(v, par=par, sw=sw, ckv=c_kv, cbe=c_bbe, ps1=c_scr1_prev):
                            gT.wait(v, ckv)
                            pvA.wait(v, max(cbe, ps1))
                            pvV.wait(v)
                            V(v, v.scalar_tensor_tensor(scr3, pK[par], 1.0,
                                                        bkb[:, sw * KSW:(sw + 1) * KSW],
                                                        OP.add, OP.add))
                        blk.vector(f_phi_v1)
                        c_s3 = pvV.n

                        def f_phi_a(s, cs3=c_s3, cmp_=c_max_prev):
                            pvV.wait(s, cs3)
                            pvV.wait(s, cmp_)
                            A(s, s.activation(scr1, scr3, AF.Relu, scale=-1.0, bias=1.0))
                            pvA.wait(s)
                            A(s, s.activation(scr1, scr1, AF.Exp, scale=-1.0))
                        blk.scalar(f_phi_a)
                        c_phi_a = pvA.n
                        c_scr1_prev = pvA.n - 1

                        def f_phi_v(v, par=par, sw=sw, ckv=c_kv, ca=c_phi_a,
                                    ktf=ktok_free[idx % 2], ktok=ktok, vtok=vtok):
                            gT.wait(v, ktf)
                            pvA.wait(v, ca)
                            pvV.wait(v)
                            V(v, v.tensor_tensor(ktok, scr1, scr3, OP.max))
                            pvV.wait(v)
                            V(v, v.tensor_tensor(vtok, pV[par], bvb[:, sw * KSW:(sw + 1) * KSW], OP.add))
                        blk.vector(f_phi_v)
                        c_tok = pvV.n
                        c_max_prev = pvV.n - 1
                        pKfree[par] = (c_phi_a, c_tok)

                        def f_kvmm(t, idx=idx, ct=c_tok, kvf=kvps_free, ktok=ktok, vtok=vtok):
                            pvV.wait(t, max(ct, kvf))
                            for h in range(NKVS):
                                ks = ktok[:, h * 128:(h + 1) * 128]
                                vs = vtok[:, h * 128:(h + 1) * 128]
                                t.matmul(pKV[:, h * 128:(h + 1) * 128], ks, vs,
                                         start=(idx == 0 and h == 0),
                                         stop=(idx == NTB - 1 and h == NKVS - 1),
                                         skip_group_check=True)
                                last = t.matmul(pKS[:, h:h + 1], ks, ones_bf[:, 0:1],
                                                start=(idx == 0 and h == 0),
                                                stop=(idx == NTB - 1 and h == NKVS - 1),
                                                skip_group_check=True)
                            gT.inc(last)
                        blk.tensor(f_kvmm)
                        ktok_free[idx % 2] = gT.n

                    def f_kvev(v, sw=sw, c=gT.n):
                        gT.wait(v, c)
                        V(v, v.tensor_copy(kvt[:, sw * NKVS * 128:(sw + 1) * NKVS * 128], pKV))
                        pvV.wait(v)
                        V(v, v.tensor_scalar(scrK, pKS, 1e-10, None, OP.add))
                        pvV.wait(v)
                        V(v, v.reciprocal(rnorm[:, sw * NKVS:(sw + 1) * NKVS], scrK))
                    blk.vector(f_kvev)
                    kvps_free = pvV.n

        PH2 = ExitStack()
        Sp2 = lambda n_, sh, dty: PH2.enter_context(nc.sbuf_tensor(n_, sh, dty)).ap()
        with PH2:
            atnT = Sp2("atnT", [128, NH * TO], BF16)
            with ExitStack() as PH:
                Sp = lambda n_, sh, dty: PH.enter_context(nc.sbuf_tensor(n_, sh, dty)).ap()
                wqbf = [Sp(f"wqbf{i}", [128, DT * 128], BF16) for i in range(2)]
                qbf = Sp("qbf", [128, TO], BF16)
                qs1 = Sp("qs1", [128, TO], F32)
                qs2 = Sp("qs2", [128, TO], F32)
                qs3 = Sp("qs3", [128, TO], F32)
                pQ = [PH.enter_context(nc.psum_tensor(f"pQ{i}", [128, TO], F32)).ap() for i in range(2)]
                pA = [PH.enter_context(nc.psum_tensor(f"pA{i}", [128, TO], F32)).ap() for i in range(2)]

                with nc.Block("p2") as blk:
                    def f_b(g):
                        SBQ.inc(g.dma_start(out=bqc, in_=bqD), 16)
                    blk.gpsimd(f_b)

                    qgrp = [0] * NH
                    agrp = [0] * NH
                    pQfree = [(0, 0)] * 2
                    pAfree = [0] * 2
                    qbf_free = 0
                    qs2_free = 0
                    qs1_prev = 0

                    for h in range(NH):
                        par = h % 2

                        def f_w(g, h=h, par=par, free=(qgrp[h - 2] if h >= 2 else 0)):
                            gT.wait(g, free)
                            SQ[par].inc(g.dma_start(out=wqbf[par], in_=wq[h].rearrange("p t f -> p (t f)")), 16)
                        blk.gpsimd(f_w)
                        w_rdy = SQ[par].n

                        def f_q(t, h=h, par=par, w_rdy=w_rdy, slot=pQfree[par]):
                            SQ[par].wait(t, w_rdy)
                            SBQ.wait(t, 16)
                            t.wait_ge(pvA.h, slot[0])
                            t.wait_ge(pvV.h, slot[1])
                            for d in range(DT):
                                lhs = wqbf[par][:, d * 128:(d + 1) * 128]
                                for qc in range(NQT):
                                    last = t.matmul(pQ[par][:, qc * 512:(qc + 1) * 512], lhs,
                                                    xnt[:, d * TO + qc * 512: d * TO + (qc + 1) * 512],
                                                    start=(d == 0), stop=(d == DT - 1))
                            gT.inc(last)
                        blk.tensor(f_q)
                        qgrp[h] = gT.n

                        def f_phi_v1(v, h=h, par=par, qg=qgrp[h], ps1=qs1_prev):
                            gT.wait(v, qg)
                            SBQ.wait(v, 16)
                            pvA.wait(v, ps1)
                            pvV.wait(v)
                            V(v, v.tensor_scalar(qs3, pQ[par], bqc[:, h:h + 1], 1.0,
                                                 OP.add, OP.add))
                        blk.vector(f_phi_v1)
                        c_s3 = pvV.n

                        def f_phi_a(s, cs3=c_s3, q2f=qs2_free):
                            pvV.wait(s, cs3)
                            pvV.wait(s, q2f)
                            A(s, s.activation(qs1, qs3, AF.Relu, scale=-1.0, bias=1.0))
                            pvA.wait(s)
                            A(s, s.activation(qs2, qs1, AF.Exp, scale=-1.0))
                        blk.scalar(f_phi_a)
                        c_phi_a = pvA.n
                        qs1_prev = pvA.n - 1

                        def f_phi_v(v, h=h, par=par, ca=c_phi_a, qbff=qbf_free):
                            pvA.wait(v, ca)
                            gT.wait(v, qbff)
                            pvV.wait(v)
                            V(v, v.tensor_tensor(qbf, qs2, qs3, OP.max))
                        blk.vector(f_phi_v)
                        c_q = pvV.n
                        qs2_free = pvV.n
                        pQfree[par] = (c_phi_a, c_q)

                        def f_att(t, h=h, par=par, cq=c_q, paf=pAfree[par]):
                            pvV.wait(t, max(cq, paf))
                            g = h // GQ
                            for qc in range(NQT):
                                last = t.matmul(pA[par][:, qc * 512:(qc + 1) * 512],
                                                kvt[:, g * 128:(g + 1) * 128],
                                                qbf[:, qc * 512:(qc + 1) * 512],
                                                start=True, stop=True)
                            gT.inc(last)
                        blk.tensor(f_att)
                        agrp[h] = gT.n
                        qbf_free = gT.n

                        def f_nrm(v, h=h, par=par, ag=agrp[h]):
                            gT.wait(v, ag)
                            V(v, v.tensor_scalar(atnT[:, h * TO:(h + 1) * TO], pA[par],
                                                 rnorm[:, (h // GQ):(h // GQ) + 1], None, OP.mult))
                        blk.vector(f_nrm)
                        pAfree[par] = pvV.n

            with ExitStack() as PH:
                Sp = lambda n_, sh, dty: PH.enter_context(nc.sbuf_tensor(n_, sh, dty)).ap()
                wobf = [Sp(f"wobf{i}", [128, DT * OCH], BF16) for i in range(2)]
                ostg = Sp("ostg", [128, NTO * OCH], BF16)
                bor = Sp("bor", [1, D], BF16)
                pO = [PH.enter_context(nc.psum_tensor(f"pO{i}", [128, OCH], F32)).ap() for i in range(4)]

                with nc.Block("p3") as blk:
                    def f_b(g):
                        SBO.inc(g.dma_start(out=bor, in_=boD), 16)
                    blk.gpsimd(f_b)

                    ogrp = {}
                    pOfree = [0] * 4
                    store = [0] * NOCH

                    for ch in range(NOCH):
                        par = ch % 2

                        PCo = DT // 4
                        if ch == 0:
                            def f_w(g):
                                for p in range(4):
                                    SOP[p].inc(g.dma_start(
                                        out=wobf[0][:, p * PCo * OCH:(p + 1) * PCo * OCH],
                                        in_=wo[0][:, p * PCo:(p + 1) * PCo, :].rearrange("p t f -> p (t f)")), 16)
                            blk.gpsimd(f_w)
                            w_rdy = [SOP[p].n for p in range(4)]
                        else:
                            def f_w(g, ch=ch, par=par,
                                    free=(ogrp[(ch - 2, NTO - 1)] if ch >= 2 else 0)):
                                gT.wait(g, free)
                                SO[par].inc(g.dma_start(out=wobf[par], in_=wo[ch].rearrange("p t f -> p (t f)")), 16)
                            blk.gpsimd(f_w)
                            w_rdy = SO[par].n

                        for tau in range(NTO):
                            k4 = (ch * NTO + tau) % 4

                            def f_o(t, ch=ch, tau=tau, par=par, k4=k4, w_rdy=w_rdy,
                                    pf=pOfree[k4], PCo=PCo):
                                if ch > 0:
                                    SO[par].wait(t, w_rdy)
                                SBO.wait(t, 16)
                                t.wait_ge(pvA.h, pf)
                                for i in range(DT):
                                    if ch == 0 and i % PCo == 0:
                                        SOP[i // PCo].wait(t, w_rdy[i // PCo])
                                    t.matmul(pO[k4], atnT[:, i * TO + tau * 128: i * TO + (tau + 1) * 128],
                                             wobf[par][:, i * OCH:(i + 1) * OCH],
                                             start=(i == 0), stop=False)
                                gT.inc(t.matmul(pO[k4], ones_bf[0:1, 0:128],
                                                bor[0:1, ch * OCH:(ch + 1) * OCH],
                                                start=False, stop=True))
                            blk.tensor(f_o)
                            ogrp[(ch, tau)] = gT.n

                            def f_ev(s, ch=ch, tau=tau, k4=k4, og=ogrp[(ch, tau)],
                                     stf=(store[ch - 1] if ch >= 1 else 0)):
                                gT.wait(s, og)
                                if tau == 0:
                                    SP3.wait(s, stf)
                                A(s, s.copy(ostg[:, tau * OCH:(tau + 1) * OCH], pO[k4]))
                            blk.scalar(f_ev)
                            pOfree[k4] = pvA.n

                        def f_st(s, ch=ch, ev=pvA.n):
                            pvA.wait(s, ev)
                            SP3.inc(s.dma_start(out=ascr[ch].rearrange("p t f -> p (t f)"), in_=ostg), 16)
                        blk.scalar(f_st)
                        store[ch] = SP3.n

        with ExitStack() as PH:
            Sp = lambda n_, sh, dty: PH.enter_context(nc.sbuf_tensor(n_, sh, dty)).ap()
            webf = [Sp(f"webf{i}", [128, DT * HCH], BF16) for i in range(2)]
            bb_sb = [Sp(f"bb_sb{i}", [128, HCH], BF16) for i in range(2)]
            acc = Sp("acc", [128, NTO * HCH], F32)
            atns = Sp("atns", [128, NTO * HCH], BF16)
            ostg5 = [Sp(f"ostg5{i}", [128, NTO * HCH], F32) for i in range(2)]
            trl = [Sp(f"trl{i}", [128, HCH], F32) for i in range(2)]
            ber = Sp("ber", [1, E * HCH], BF16)
            pE = [PH.enter_context(nc.psum_tensor(f"pE{i}", [128, HCH], F32)).ap() for i in range(4)]
            pBB = PH.enter_context(nc.psum_tensor("pBB", [128, HCH], F32)).ap()
            HOC = HCH // OCH

            with nc.Block("p5") as blk:
                p3_done = SP3.n

                def f_pro_s(s):
                    SP3.wait(s, p3_done)
                blk.sync(f_pro_s)

                def f_pro_a(s):
                    SP3.wait(s, p3_done)
                blk.scalar(f_pro_a)

                def f_pro_g(g):
                    SP3.wait(g, p3_done)
                blk.gpsimd(f_pro_g)

                egrp = {}
                bbev = {}
                bb_free = [0] * 2
                pEfree = [0] * 4
                trl_free = [0] * 2
                outst = [0] * NCH
                combine_done = 0
                ber_free = 0

                for ch in range(NCH):
                    spar = ch % 2

                    def f_x(s, ch=ch, spar=spar, of=(outst[ch - 2] if ch >= 2 else 0)):
                        SOUT[spar].wait(s, of)
                        SXF[spar].inc(s.dma_start(out=ostg5[spar], in_=xoc[ch].rearrange("p t f -> p (t f)")), 16)
                    blk.sync(f_x)
                    xfill = SXF[spar].n

                    def f_at(s, ch=ch, cd=combine_done):
                        pvV.wait(s, cd)
                        for oc in range(HOC):
                            SAT.inc(s.dma_start(
                                out=atns.rearrange("p (t f) -> p t f", t=NTO)[:, :, oc * OCH:(oc + 1) * OCH],
                                in_=ascr[ch * HOC + oc]), 16)
                    blk.scalar(f_at)
                    atld = SAT.n

                    def f_be(g, ch=ch, bf=ber_free):
                        gT.wait(g, bf)
                        SBE.inc(g.dma_start(out=ber, in_=beD[ch:ch + 1, :]), 16)
                    blk.gpsimd(f_be)
                    ber_rdy = SBE.n

                    for e in range(E):
                        wpar = e % 2
                        k2 = ch * E + e

                        PCe = DT // 4
                        if k2 == 0:
                            def f_w(g):
                                for p in range(4):
                                    SEP[p].inc(g.dma_start(
                                        out=webf[0][:, p * PCe * HCH:(p + 1) * PCe * HCH],
                                        in_=we[0, 0][:, p * PCe:(p + 1) * PCe, :].rearrange("p t f -> p (t f)")), 16)
                            blk.gpsimd(f_w)
                            w_rdy = [SEP[p].n for p in range(4)]
                        else:
                            def f_w(g, ch=ch, e=e, wpar=wpar,
                                    free=(egrp[(ch * E + e - 2, NTO - 1)] if k2 >= 2 else 0)):
                                gT.wait(g, free)
                                SE[wpar].inc(g.dma_start(out=webf[wpar], in_=we[ch, e].rearrange("p t f -> p (t f)")), 16)
                            blk.gpsimd(f_w)
                            w_rdy = SE[wpar].n

                        def f_bb(t, ch=ch, e=e, ber_rdy=ber_rdy, bbe=(bbev.get((ch * E + e - 1,), 0))):
                            SBE.wait(t, ber_rdy)
                            t.wait_ge(pvA.h, bbe)
                            gT.inc(t.matmul(pBB, ones_bf[0:1, 0:128],
                                            ber[0:1, e * HCH:(e + 1) * HCH], start=True, stop=True))
                        blk.tensor(f_bb)
                        c_bb = gT.n
                        if e == E - 1:
                            ber_free_new = c_bb

                        def f_bbe(s, ch=ch, e=e, wpar=wpar, cb=c_bb, bf=bb_free[wpar]):
                            gT.wait(s, cb)
                            pvV.wait(s, bf)
                            A(s, s.copy(bb_sb[wpar], pBB))
                        blk.scalar(f_bbe)
                        bbev[(ch * E + e,)] = pvA.n
                        bb_rdy = pvA.n

                        for tau in range(NTO):
                            k4 = (k2 * NTO + tau) % 4

                            def f_e(t, ch=ch, e=e, tau=tau, wpar=wpar, k4=k4, k2=k2,
                                    w_rdy=w_rdy, pf=pEfree[k4], PCe=PCe):
                                if k2 > 0:
                                    SE[wpar].wait(t, w_rdy)
                                t.wait_ge(pvV.h, pf)
                                for d in range(DT):
                                    if k2 == 0 and d % PCe == 0:
                                        SEP[d // PCe].wait(t, w_rdy[d // PCe])
                                    last = t.matmul(pE[k4],
                                                    xnt[:, d * TO + tau * 128: d * TO + (tau + 1) * 128],
                                                    webf[wpar][:, d * HCH:(d + 1) * HCH],
                                                    start=(d == 0), stop=(d == DT - 1))
                                gT.inc(last)
                            blk.tensor(f_e)
                            egrp[(k2, tau)] = gT.n

                            def f_r(v, ch=ch, e=e, tau=tau, k4=k4, wpar=wpar, eg=egrp[(k2, tau)],
                                    cd=combine_done, tf=trl_free[tau % 2], br=bb_rdy):
                                gT.wait(v, eg)
                                pvA.wait(v, br)
                                if e == 0 and tau == 0:
                                    pvV.wait(v, max(cd, mask_done))
                                pvV.wait(v, tf)
                                V(v, v.tensor_tensor(trl[tau % 2], pE[k4], bb_sb[wpar], OP.add))
                                pvV.wait(v)
                                if e == 0:
                                    V(v, v.tensor_scalar(acc[:, tau * HCH:(tau + 1) * HCH],
                                                         trl[tau % 2],
                                                         mask[:, tau * E:tau * E + 1], 0.0,
                                                         OP.mult, OP.max))
                                else:
                                    V(v, v.tensor_scalar(trl[tau % 2], trl[tau % 2],
                                                         mask[:, tau * E + e:tau * E + e + 1], 0.0,
                                                         OP.mult, OP.max))
                                    pvV.wait(v)
                                    V(v, v.tensor_tensor(acc[:, tau * HCH:(tau + 1) * HCH],
                                                         trl[tau % 2],
                                                         acc[:, tau * HCH:(tau + 1) * HCH], OP.add))
                            blk.vector(f_r)
                            pEfree[k4] = pvV.n
                            trl_free[tau % 2] = pvV.n
                            if tau == NTO - 1:
                                bb_free[wpar] = pvV.n

                            if e == E - 1:
                                def f_cmb(v, ch=ch, tau=tau, spar=spar, xfill=xfill, atld=atld):
                                    SXF[spar].wait(v, xfill)
                                    SAT.wait(v, atld)
                                    sl = slice(tau * HCH, (tau + 1) * HCH)
                                    pvV.wait(v)
                                    V(v, v.tensor_tensor(trl[tau % 2], acc[:, sl], atns[:, sl], OP.add))
                                    pvV.wait(v)
                                    V(v, v.tensor_tensor(ostg5[spar][:, sl], trl[tau % 2],
                                                         ostg5[spar][:, sl], OP.add))
                                blk.vector(f_cmb)
                                trl_free[tau % 2] = pvV.n
                    ber_free = ber_free_new
                    combine_done = pvV.n

                    def f_out(s, ch=ch, spar=spar, cd=combine_done):
                        pvV.wait(s, cd)
                        SOUT[spar].inc(s.dma_start(
                            out=out.rearrange("(t p) n -> p t n", p=128)[:, :, ch * HCH:(ch + 1) * HCH],
                            in_=ostg5[spar].rearrange("p (t f) -> p t f", t=NTO)), 16)
                    blk.sync(f_out)
                    outst[ch] = SOUT[spar].n

                def f_fin(s):
                    for chn in [SOUT[0], SOUT[1], SXF[0], SXF[1], SXA[0], SXA[1],
                                SXB[0], SXB[1], SAT, SP3,
                                SPRE, SKV, SQ[0], SQ[1], SBQ, SO[0], SO[1], SBO,
                                SE[0], SE[1], SBE,
                                STR[0][0], STR[0][1], STR[1][0], STR[1][1],
                                SKP[0], SKP[1], SKP[2], SKP[3], SVP[0], SVP[1], SVP[2], SVP[3],
                                SOP[0], SOP[1], SOP[2], SOP[3], SEP[0], SEP[1], SEP[2], SEP[3]]:
                        chn.wait(s)
                blk.sync(f_fin)

    return nc


def kernel(**inputs):
    c = mk_cfg()
    D, TO, DT, NTO = c["D"], c["TO"], c["DT"], c["NTO"]
    NCH, HCH, NOCH, OCH, KSW, NH, E, KD = (c["NCH"], c["HCH"], c["NOCH"], c["OCH"],
                                           c["KSW"], c["NH"], c["E"], c["KD"])
    f32 = np.float32
    Wq = np.asarray(inputs["L0_attn_Wq"], f32)
    Wk = np.asarray(inputs["L0_attn_Wk"], f32)
    Wv = np.asarray(inputs["L0_attn_Wv"], f32)
    Wo = np.asarray(inputs["L0_attn_Wo"], f32)
    We = np.asarray(inputs["L0_ffn_W_e"], f32)
    Wg = np.asarray(inputs["L0_ffn_W_gate"], f32)

    wq_h = np.ascontiguousarray(Wq.T.reshape(DT, 128, NH, 128).transpose(2, 1, 0, 3))
    wk_h = np.ascontiguousarray(Wk.T.reshape(DT, 128, 2, KSW).transpose(2, 1, 0, 3))
    wv_h = np.ascontiguousarray(Wv.T.reshape(DT, 128, 2, KSW).transpose(2, 1, 0, 3))
    wo_h = np.ascontiguousarray(Wo.T.reshape(DT, 128, NOCH, OCH).transpose(2, 1, 0, 3))
    we_h = np.ascontiguousarray(We.reshape(E, DT, 128, NCH, HCH).transpose(3, 0, 2, 1, 4))
    wg_h = np.ascontiguousarray(Wg.reshape(DT, 128, E).transpose(1, 0, 2))
    bq_h = np.ascontiguousarray(np.asarray(inputs["L0_attn_bq"], f32).reshape(NH, 128).T)
    bk_h = np.asarray(inputs["L0_attn_bk"], f32).reshape(1, KD)
    bv_h = np.asarray(inputs["L0_attn_bv"], f32).reshape(1, KD)
    bo_h = np.asarray(inputs["L0_attn_bo"], f32).reshape(1, D)
    bg_h = np.asarray(inputs["L0_ffn_b_gate"], f32).reshape(E, 1)
    be_h = np.ascontiguousarray(
        np.asarray(inputs["L0_ffn_b_e"], f32).reshape(E, NCH, HCH).transpose(1, 0, 2)
    ).reshape(NCH, E * HCH)

    x = np.asarray(inputs["L0_attn_hidden_states"], f32)
    nc = build_program(c)

    in_maps = []
    for core in range(8):
        b, half = core // 2, core % 2
        xown = np.ascontiguousarray(x[b, half * TO:(half + 1) * TO, :])
        xoth = np.ascontiguousarray(x[b, (1 - half) * TO:(2 - half) * TO, :])
        xoc = np.ascontiguousarray(xown.reshape(NTO, 128, NCH, HCH).transpose(2, 1, 0, 3))
        in_maps.append({
            "xo": xown, "xoc": xoc, "xb": xoth,
            "wq": wq_h, "wk": wk_h, "wv": wv_h, "wo": wo_h, "we": we_h, "wg": wg_h,
            "bq": bq_h, "bk": bk_h, "bv": bv_h, "bo": bo_h, "bg": bg_h, "be": be_h,
        })

    res = run_bass_kernel_spmd(nc, in_maps, core_ids=list(range(8)))
    full = np.empty((4, 2048, 4096), dtype=np.float32)
    for core in range(8):
        b, half = core // 2, core % 2
        full[b, half * TO:(half + 1) * TO, :] = res.results[core]["out"]
    return full


# version 21 (final)
# speedup vs baseline: 1.1907x; 1.0140x over previous
import numpy as np
import ml_dtypes
import concourse.bass as bass
import concourse.mybir as mybir
from concourse.alu_op_type import AluOpType as OP
from concourse.bass_utils import run_bass_kernel_spmd
from contextlib import ExitStack

F32 = mybir.dt.float32
BF16 = mybir.dt.bfloat16
AF = mybir.ActivationFunctionType


def mk_cfg(D=4096, TO=1024, NH=32, NKV=8, E=8, HCH=512, OCH=256):
    c = {}
    c["D"] = D; c["DT"] = D // 128
    c["TO"] = TO; c["NTO"] = TO // 128
    c["TB"] = 2 * TO; c["NTB"] = 2 * TO // 128
    c["NH"] = NH; c["NKV"] = NKV; c["GQ"] = NH // NKV
    c["KD"] = NKV * 128
    c["E"] = E
    c["HCH"] = HCH; c["NCH"] = D // HCH
    c["OCH"] = OCH; c["NOCH"] = D // OCH
    c["KSW"] = c["KD"] // 2
    c["NKVS"] = NKV // 2
    c["NQT"] = TO // 512
    assert NH * 128 == D and TO % 512 == 0 and NKV % 2 == 0
    return c


class Chan:
    def __init__(self, nc, name):
        self.h = nc.alloc_semaphore(name)
        self.n = 0

    def inc(self, instr, k=1):
        instr.then_inc(self.h, k)
        self.n += k
        return self.n

    def wait(self, eng, val=None):
        eng.wait_ge(self.h, self.n if val is None else val)


def build_program(c):
    nc = bass.Bass("TRN2", target_bir_lowering=False)
    D, DT, TO, NTO, TB, NTB = c["D"], c["DT"], c["TO"], c["NTO"], c["TB"], c["NTB"]
    KD, NH, NKV, GQ, E = c["KD"], c["NH"], c["NKV"], c["GQ"], c["E"]
    HCH, NCH, OCH, NOCH = c["HCH"], c["NCH"], c["OCH"], c["NOCH"]
    KSW, NKVS, NQT = c["KSW"], c["NKVS"], c["NQT"]

    dt = nc.dram_tensor
    xo = dt("xo", [TO, D], F32, kind="ExternalInput").ap()
    xoc = dt("xoc", [NCH, 128, NTO, HCH], F32, kind="ExternalInput").ap()
    xb = dt("xb", [TO, D], F32, kind="ExternalInput").ap()
    wq = dt("wq", [NH, 128, DT, 128], F32, kind="ExternalInput").ap()
    wk = dt("wk", [2, 128, DT, KSW], F32, kind="ExternalInput").ap()
    wv = dt("wv", [2, 128, DT, KSW], F32, kind="ExternalInput").ap()
    wo = dt("wo", [NOCH, 128, DT, OCH], F32, kind="ExternalInput").ap()
    we = dt("we", [NCH, E, 128, DT, HCH], F32, kind="ExternalInput").ap()
    wg = dt("wg", [128, DT, E], F32, kind="ExternalInput").ap()
    bqD = dt("bq", [128, NH], F32, kind="ExternalInput").ap()
    bkD = dt("bk", [1, KD], F32, kind="ExternalInput").ap()
    bvD = dt("bv", [1, KD], F32, kind="ExternalInput").ap()
    boD = dt("bo", [1, D], F32, kind="ExternalInput").ap()
    bgD = dt("bg", [E, 1], F32, kind="ExternalInput").ap()
    beD = dt("be", [NCH, E * HCH], F32, kind="ExternalInput").ap()
    out = dt("out", [TO, D], F32, kind="ExternalOutput").ap()
    ascr = dt("ascr", [NOCH, 128, NTO, OCH], BF16, kind="Internal").ap()

    id_bf_h = nc.inline_tensor(np.eye(128, dtype=ml_dtypes.bfloat16), name="id_bf_h")
    id_f_h = nc.inline_tensor(np.eye(E, dtype=np.float32), name="id_f_h")

    PERS = ExitStack()
    S = lambda name, shape, dty: PERS.enter_context(nc.sbuf_tensor(name, shape, dty)).ap()
    with PERS:
        xnt = S("xnt", [128, DT * TO], BF16)
        kvt = S("kvt", [128, NKV * 128], BF16)
        rnorm = S("rnorm", [128, NKV], F32)
        logits = S("logits", [128, NTO * E], F32)
        mask = S("mask", [128, NTO * E], F32)
        ones_bf = S("ones_bf", [128, 128], BF16)
        idbf = S("idbf", [128, 128], BF16)
        idf = S("idf", [E, E], F32)
        wg_f = S("wg_f", [128, DT * E], F32)
        wg_hi = S("wg_hi", [128, DT * E], BF16)
        wg_lo = S("wg_lo", [128, DT * E], BF16)
        bg_c = S("bg_c", [E, 1], F32)
        bqc = S("bqc", [128, NH], F32)
        bkb = S("bkb", [128, KD], BF16)
        bvb = S("bvb", [128, KD], BF16)
        st6 = S("st6", [128, 6 * (D // 512)], F32)
        st2 = S("st2", [128, 2], F32)
        mv = S("mv", [128, 4], F32)
        t8 = S("t8", [128, 8], F32)

        SXA = [Chan(nc, "SXA0"), Chan(nc, "SXA1")]
        SXB = [Chan(nc, "SXB0"), Chan(nc, "SXB1")]
        SXF = [Chan(nc, "SXF0"), Chan(nc, "SXF1")]
        SOUT = [Chan(nc, "SOUT0"), Chan(nc, "SOUT1")]
        SPRE = Chan(nc, "SPRE")
        SKV = Chan(nc, "SKV")
        SQ = [Chan(nc, "SQ0"), Chan(nc, "SQ1")]
        SBQ = Chan(nc, "SBQ")
        SO = [Chan(nc, "SO0"), Chan(nc, "SO1")]
        SBO = Chan(nc, "SBO")
        SE = [Chan(nc, "SE0"), Chan(nc, "SE1")]
        SBE = Chan(nc, "SBE")
        SKP = [Chan(nc, f"SKP{i}") for i in range(4)]
        SVP = [Chan(nc, f"SVP{i}") for i in range(4)]
        SOP = [Chan(nc, f"SOP{i}") for i in range(4)]
        SEP = [Chan(nc, f"SEP{i}") for i in range(4)]
        SP3 = Chan(nc, "SP3")
        SAT = Chan(nc, "SAT")
        pvV = Chan(nc, "pvV")
        pvA = Chan(nc, "pvA")
        gT = Chan(nc, "gT")

        def V(v, instr):
            return pvV.inc(instr)

        def A(s, instr):
            return pvA.inc(instr)

        with ExitStack() as PH:
            Sp = lambda n_, sh, dty: PH.enter_context(nc.sbuf_tensor(n_, sh, dty)).ap()
            xbufs = [Sp("xbufA", [128, D], F32), Sp("xbufB", [128, D], F32)]
            xnfs = [Sp("xnfA", [128, D], F32), Sp("xnfB", [128, D], F32)]
            xhis = [Sp("xhiA", [128, D], BF16), Sp("xhiB", [128, D], BF16)]
            xlos = [Sp("xloA", [128, D], BF16), Sp("xloB", [128, D], BF16)]
            lots = [Sp("lotA", [128, D], BF16), Sp("lotB", [128, D], BF16)]
            glog_all = Sp("glog_all", [E, NTO * 128], F32)
            ptH = [PH.enter_context(nc.psum_tensor(f"ptH{i}", [128, 128], BF16)).ap() for i in range(2)]
            ptL = [PH.enter_context(nc.psum_tensor(f"ptL{i}", [128, 128], BF16)).ap() for i in range(2)]
            pG = [PH.enter_context(nc.psum_tensor(f"pG{i}", [E, 128], F32)).ap() for i in range(2)]
            pGT = [PH.enter_context(nc.psum_tensor(f"pGT{i}", [128, E], F32)).ap() for i in range(2)]

            with nc.Block("a0") as blk:
                def prelude_g(g):
                    SPRE.inc(g.dma_start(out=idbf, in_=id_bf_h.ap()), 16)
                    SPRE.inc(g.dma_start(out=idf, in_=id_f_h.ap()), 16)
                    SPRE.inc(g.dma_start(out=wg_f, in_=wg.rearrange("p t e -> p (t e)")), 16)
                    SPRE.inc(g.dma_start(out=bg_c, in_=bgD), 16)
                blk.gpsimd(prelude_g)
                consts_rdy = SPRE.n

                def prelude_v(v):
                    SPRE.wait(v, consts_rdy)
                    V(v, v.memset(ones_bf, 1.0))
                    pvV.wait(v)
                    V(v, v.tensor_copy(wg_hi, wg_f))
                    pvV.wait(v)
                    V(v, v.tensor_tensor(wg_lo, wg_f, wg_hi, OP.subtract))
                blk.vector(prelude_v)
                wg_rdy = pvV.n

                xbuf_free = [0, 0]
                xn_free = [0, 0]
                gate_done = [0] * NTO
                pG_free = 0
                NB = D // 512

                for tau in range(NTO):
                    xbuf = xbufs[tau % 2]
                    xnf, xhi, xlo, lot = xnfs[tau % 2], xhis[tau % 2], xlos[tau % 2], lots[tau % 2]
                    def f_sx(s, tau=tau, xf=xbuf_free[tau % 2], xbuf=xbuf):
                        pvV.wait(s, xf)
                        SXA[tau % 2].inc(s.dma_start(out=xbuf, in_=xo[tau * 128:(tau + 1) * 128, :]), 16)
                    blk.sync(f_sx)
                    x_rdy = SXA[tau % 2].n

                    def f_ln1(v, x_rdy=x_rdy, xbuf=xbuf, tau=tau):
                        SXA[tau % 2].wait(v, x_rdy)
                        for i in range(NB):
                            pvV.wait(v)
                            V(v, v.bn_stats(st6[:, 6 * i:6 * (i + 1)], xbuf[:, 512 * i:512 * (i + 1)]))
                        pvV.wait(v)
                        V(v, v.bn_aggr(st2, st6))
                        pvV.wait(v)
                        V(v, v.tensor_scalar(mv[:, 2:3], st2[:, 1:2], 1e-5, None, OP.add))
                    blk.vector(f_ln1)
                    c_ln1 = pvV.n

                    def f_sq(s, c=c_ln1):
                        pvV.wait(s, c)
                        A(s, s.activation(mv[:, 3:4], mv[:, 2:3], AF.Ln))
                        pvA.wait(s)
                        A(s, s.activation(mv[:, 1:2], mv[:, 3:4], AF.Exp, scale=-0.5))
                    blk.scalar(f_sq)
                    c_sqrt = pvA.n

                    def f_ln2(v, c=c_sqrt, xnfree=xn_free[tau % 2], xbuf=xbuf,
                              xnf=xnf, xhi=xhi, xlo=xlo):
                        pvA.wait(v, c)
                        gT.wait(v, xnfree)
                        V(v, v.tensor_scalar(xnf, xbuf, st2[:, 0:1], mv[:, 1:2], OP.subtract, OP.mult))
                        pvV.wait(v)
                        V(v, v.tensor_copy(xhi, xnf))
                        pvV.wait(v)
                        V(v, v.tensor_tensor(xlo, xnf, xhi, OP.subtract))
                    blk.vector(f_ln2)
                    xbuf_free[tau % 2] = pvV.n
                    c_xn = pvV.n

                    baseA = pvA.n
                    th = [0] * DT
                    tl = [0] * DT

                    def f_tr(t, tau=tau, c_xn=c_xn, baseA=baseA, xhi=xhi, xlo=xlo):
                        pvV.wait(t, c_xn)
                        for j in range(DT):
                            if j >= 2:
                                t.wait_ge(pvA.h, baseA + 2 * j - 3)
                            th[j] = gT.inc(t.transpose(ptH[j % 2], xhi[:, j * 128:(j + 1) * 128], idbf))
                            if j >= 2:
                                t.wait_ge(pvA.h, baseA + 2 * j - 2)
                            tl[j] = gT.inc(t.transpose(ptL[j % 2], xlo[:, j * 128:(j + 1) * 128], idbf))
                    blk.tensor(f_tr)
                    xn_free[tau % 2] = gT.n

                    def f_ev(s, tau=tau, lot=lot):
                        for j in range(DT):
                            gT.wait(s, th[j])
                            A(s, s.copy(xnt[:, j * TO + tau * 128: j * TO + (tau + 1) * 128], ptH[j % 2]))
                            gT.wait(s, tl[j])
                            A(s, s.copy(lot[:, j * 128:(j + 1) * 128], ptL[j % 2]))
                    blk.scalar(f_ev)
                    assert pvA.n == baseA + 2 * DT
                    ev_done_A = pvA.n
                    ev_done_V = pvV.n

                    def f_gate(t, tau=tau, a=ev_done_A, vv=ev_done_V, pf=pG_free, lot=lot):
                        pvA.wait(t, max(a, pf))
                        pvV.wait(t, max(vv, wg_rdy))
                        SPRE.wait(t, consts_rdy)
                        for d in range(DT):
                            xs = xnt[:, d * TO + tau * 128: d * TO + (tau + 1) * 128]
                            ls = lot[:, d * 128:(d + 1) * 128]
                            pg = pG[tau % 2]
                            t.matmul(pg, wg_hi[:, d * E:(d + 1) * E], xs, start=(d == 0), stop=False)
                            t.matmul(pg, wg_lo[:, d * E:(d + 1) * E], xs, start=False, stop=False)
                            last = t.matmul(pg, wg_hi[:, d * E:(d + 1) * E], ls, start=False,
                                            stop=(d == DT - 1))
                        gT.inc(last)
                    blk.tensor(f_gate)
                    gate_done[tau] = gT.n

                    def emit_gcopy(tau2):
                        def f_gc(s, tau2=tau2, c=gate_done[tau2]):
                            gT.wait(s, c)
                            A(s, s.copy(glog_all[:, tau2 * 128:(tau2 + 1) * 128], pG[tau2 % 2]))
                        blk.scalar(f_gc)
                        return pvA.n
                    if tau >= 1:
                        gc = emit_gcopy(tau - 1)
                        if tau >= 1:
                            pG_free_next = gc
                    if tau >= 1:
                        pG_free = pG_free_prev if tau >= 2 else 0
                        pG_free_prev = pG_free_next
                pG_free = emit_gcopy(NTO - 1)
                gcopies = pvA.n

                def f_bg(v, c=gcopies):
                    pvA.wait(v, c)
                    V(v, v.tensor_scalar(glog_all, glog_all, bg_c, None, OP.add))
                blk.vector(f_bg)
                c_bg = pvV.n

                gtr = [0] * NTO
                def f_gtr_all(t, c=c_bg):
                    pvV.wait(t, c)
                    for tau in range(NTO):
                        if tau >= 2:
                            t.wait_ge(pvV.h, c + 3 * (tau - 1))
                        gtr[tau] = gT.inc(t.transpose(pGT[tau % 2],
                                                      glog_all[:, tau * 128:(tau + 1) * 128],
                                                      idf))
                blk.tensor(f_gtr_all)

                def f_msk_all(v, c=c_bg):
                    for tau in range(NTO):
                        gT.wait(v, gtr[tau])
                        V(v, v.tensor_copy(logits[:, tau * E:(tau + 1) * E], pGT[tau % 2]))
                        pvV.wait(v)
                        V(v, v.max(out=t8, in_=logits[:, tau * E:(tau + 1) * E]))
                        pvV.wait(v)
                        V(v, v.tensor_scalar(mask[:, tau * E:(tau + 1) * E],
                                             logits[:, tau * E:(tau + 1) * E],
                                             t8[:, 1:2], 0.5, OP.is_ge, OP.mult))
                    assert pvV.n == c + 3 * NTO
                blk.vector(f_msk_all)
            mask_done = pvV.n

        with ExitStack() as PH:
            Sp = lambda n_, sh, dty: PH.enter_context(nc.sbuf_tensor(n_, sh, dty)).ap()
            wkbf = Sp("wkbf", [128, DT * KSW], BF16)
            wvbf = Sp("wvbf", [128, DT * KSW], BF16)
            xbuf1s = [Sp("xbuf1A", [128, D], F32), Sp("xbuf1B", [128, D], F32)]
            xhi1s = [Sp("xhi1A", [128, D], BF16), Sp("xhi1B", [128, D], BF16)]
            xntt = Sp("xntt", [128, D], BF16)
            ktoks = [Sp("ktokA", [128, KSW], BF16), Sp("ktokB", [128, KSW], BF16)]
            vtoks = [Sp("vtokA", [128, KSW], BF16), Sp("vtokB", [128, KSW], BF16)]
            scr1 = Sp("scr1", [128, KSW], F32)
            scr3 = Sp("scr3", [128, KSW], F32)
            scrK = Sp("scrK", [128, NKVS], F32)
            bkr = Sp("bkr", [1, KD], BF16)
            bvr = Sp("bvr", [1, KD], BF16)
            pK = [PH.enter_context(nc.psum_tensor(f"pK{i}", [128, KSW], F32)).ap() for i in range(2)]
            pV = [PH.enter_context(nc.psum_tensor(f"pV{i}", [128, KSW], F32)).ap() for i in range(2)]
            pKV = PH.enter_context(nc.psum_tensor("pKV", [128, NKVS * 128], F32)).ap()
            pKS = PH.enter_context(nc.psum_tensor("pKS", [128, NKVS], F32)).ap()
            ptC = [PH.enter_context(nc.psum_tensor(f"ptC{i}", [128, 128], BF16)).ap() for i in range(2)]

            with nc.Block("a1") as blk:
                def f_b(g):
                    SKV.inc(g.dma_start(out=bkr, in_=bkD), 16)
                    SKV.inc(g.dma_start(out=bvr, in_=bvD), 16)
                blk.gpsimd(f_b)

                CWb = min(512, KSW)
                bb_chunks = [(off, min(CWb, KD - off)) for off in range(0, KD, CWb)]

                def f_bb(t):
                    SKV.wait(t, 32)
                    SPRE.wait(t, consts_rdy)
                    for i, (off, w) in enumerate(bb_chunks):
                        t.matmul(pK[i % 2][:, 0:w], ones_bf[0:1, 0:128],
                                 bkr[0:1, off:off + w], start=True, stop=True)
                        gT.inc(t.matmul(pV[i % 2][:, 0:w], ones_bf[0:1, 0:128],
                                        bvr[0:1, off:off + w], start=True, stop=True))
                blk.tensor(f_bb)
                c_bbmm = gT.n

                def f_bbe(s):
                    gT.wait(s, c_bbmm)
                    for i, (off, w) in enumerate(bb_chunks):
                        pvA.wait(s)
                        A(s, s.copy(bkb[:, off:off + w], pK[i % 2][:, 0:w]))
                        pvA.wait(s)
                        A(s, s.copy(bvb[:, off:off + w], pV[i % 2][:, 0:w]))
                blk.scalar(f_bbe)
                c_bbe = pvA.n

                kw_free = 0
                kvps_free = 0
                xbuf1_free = [0, 0]
                xhi1_free = [0, 0]
                ktok_free = [0, 0]
                c_max_prev = 0
                c_scr1_prev = 0
                pKfree = [(c_bbe, 0)] * 2

                for sw in range(2):
                    PC = DT // 4
                    def f_w(g, sw=sw, kwf=kw_free):
                        gT.wait(g, kwf)
                        for p in range(4):
                            SKP[p].inc(g.dma_start(
                                out=wkbf[:, p * PC * KSW:(p + 1) * PC * KSW],
                                in_=wk[sw][:, p * PC:(p + 1) * PC, :].rearrange("p t f -> p (t f)")), 16)
                            SVP[p].inc(g.dma_start(
                                out=wvbf[:, p * PC * KSW:(p + 1) * PC * KSW],
                                in_=wv[sw][:, p * PC:(p + 1) * PC, :].rearrange("p t f -> p (t f)")), 16)
                    blk.gpsimd(f_w)
                    w_rdy = [(SKP[p].n, SVP[p].n) for p in range(4)]

                    def emit_ln(ot):
                        xbuf1 = xbuf1s[ot % 2]
                        xhi1 = xhi1s[ot % 2]

                        def f_sx(s, ot=ot, xf=xbuf1_free[ot % 2], xbuf1=xbuf1):
                            pvV.wait(s, xf)
                            SXB[ot % 2].inc(s.dma_start(out=xbuf1, in_=xb[ot * 128:(ot + 1) * 128, :]), 16)
                        blk.sync(f_sx)
                        x_rdy = SXB[ot % 2].n

                        def f_ln1(v, x_rdy=x_rdy, xbuf1=xbuf1, ot=ot):
                            SXB[ot % 2].wait(v, x_rdy)
                            for i in range(D // 512):
                                pvV.wait(v)
                                V(v, v.bn_stats(st6[:, 6 * i:6 * (i + 1)], xbuf1[:, 512 * i:512 * (i + 1)]))
                            pvV.wait(v)
                            V(v, v.bn_aggr(st2, st6))
                            pvV.wait(v)
                            V(v, v.tensor_scalar(mv[:, 2:3], st2[:, 1:2], 1e-5, None, OP.add))
                        blk.vector(f_ln1)
                        c_ln1 = pvV.n

                        def f_sq(s, ccc=c_ln1):
                            pvV.wait(s, ccc)
                            A(s, s.activation(mv[:, 3:4], mv[:, 2:3], AF.Ln))
                            pvA.wait(s)
                            A(s, s.activation(mv[:, 1:2], mv[:, 3:4], AF.Exp, scale=-0.5))
                        blk.scalar(f_sq)
                        c_sqrt = pvA.n

                        def f_ln2(v, ccc=c_sqrt, xhf=xhi1_free[ot % 2], xbuf1=xbuf1, xhi1=xhi1):
                            pvA.wait(v, ccc)
                            gT.wait(v, xhf)
                            V(v, v.tensor_scalar(xhi1, xbuf1, st2[:, 0:1], mv[:, 1:2],
                                                 OP.subtract, OP.mult))
                        blk.vector(f_ln2)
                        xbuf1_free[ot % 2] = pvV.n
                        return pvV.n

                    def emit_tr(ot, c_xn):
                        xhi1 = xhi1s[ot % 2]
                        baseA = pvA.n
                        tt = [0] * DT

                        def f_tr(t, c_xn=c_xn, baseA=baseA, xhi1=xhi1):
                            pvV.wait(t, c_xn)
                            for j in range(DT):
                                if j >= 2:
                                    t.wait_ge(pvA.h, baseA + (j - 1))
                                tt[j] = gT.inc(t.transpose(ptC[j % 2], xhi1[:, j * 128:(j + 1) * 128], idbf))
                        blk.tensor(f_tr)
                        xhi1_free[ot % 2] = gT.n

                        def f_ev(s):
                            for j in range(DT):
                                gT.wait(s, tt[j])
                                A(s, s.copy(xntt[:, j * 128:(j + 1) * 128], ptC[j % 2]))
                        blk.scalar(f_ev)
                        assert pvA.n == baseA + DT
                        return pvA.n

                    ln_done = {}
                    for idx, bt in enumerate([x for pr in zip(range(NTO), range(NTO, NTB)) for x in pr]):
                        own = bt < NTO
                        if own and bt < NTO:
                            ln_done[bt] = emit_ln(bt)
                        if not own:
                            xntt_rdy = emit_tr(bt - NTO, ln_done[bt - NTO])
                        else:
                            xntt_rdy = 0
                        par = idx % 2

                        def f_kv(t, bt=bt, own=own, par=par, sw=sw, w_rdy=w_rdy,
                                 xntt_rdy=xntt_rdy, slot=pKfree[par], PC=PC):
                            if not own:
                                pvA.wait(t, xntt_rdy)
                            t.wait_ge(pvA.h, slot[0])
                            t.wait_ge(pvV.h, slot[1])
                            for d in range(DT):
                                if d % PC == 0:
                                    SKP[d // PC].wait(t, w_rdy[d // PC][0])
                                if own:
                                    xs = xnt[:, d * TO + bt * 128: d * TO + (bt + 1) * 128]
                                else:
                                    xs = xntt[:, d * 128:(d + 1) * 128]
                                t.matmul(pK[par], xs, wkbf[:, d * KSW:(d + 1) * KSW],
                                         start=(d == 0), stop=(d == DT - 1))
                            for d in range(DT):
                                if d % PC == 0:
                                    SVP[d // PC].wait(t, w_rdy[d // PC][1])
                                if own:
                                    xs = xnt[:, d * TO + bt * 128: d * TO + (bt + 1) * 128]
                                else:
                                    xs = xntt[:, d * 128:(d + 1) * 128]
                                last = t.matmul(pV[par], xs, wvbf[:, d * KSW:(d + 1) * KSW],
                                                start=(d == 0), stop=(d == DT - 1))
                            gT.inc(last)
                        blk.tensor(f_kv)
                        c_kv = gT.n
                        if idx == NTB - 1:
                            kw_free = gT.n

                        ktok = ktoks[idx % 2]
                        vtok = vtoks[idx % 2]

                        def f_phi_v1(v, par=par, sw=sw, ckv=c_kv, cbe=c_bbe, ps1=c_scr1_prev):
                            gT.wait(v, ckv)
                            pvA.wait(v, max(cbe, ps1))
                            pvV.wait(v)
                            V(v, v.scalar_tensor_tensor(scr3, pK[par], 1.0,
                                                        bkb[:, sw * KSW:(sw + 1) * KSW],
                                                        OP.add, OP.add))
                        blk.vector(f_phi_v1)
                        c_s3 = pvV.n

                        def f_phi_a(s, cs3=c_s3, cmp_=c_max_prev):
                            pvV.wait(s, cs3)
                            pvV.wait(s, cmp_)
                            A(s, s.activation(scr1, scr3, AF.Relu, scale=-1.0, bias=1.0))
                            pvA.wait(s)
                            A(s, s.activation(scr1, scr1, AF.Exp, scale=-1.0))
                        blk.scalar(f_phi_a)
                        c_phi_a = pvA.n
                        c_scr1_prev = pvA.n - 1

                        def f_phi_v(v, par=par, sw=sw, ckv=c_kv, ca=c_phi_a,
                                    ktf=ktok_free[idx % 2], ktok=ktok, vtok=vtok):
                            gT.wait(v, ktf)
                            pvA.wait(v, ca)
                            pvV.wait(v)
                            V(v, v.tensor_tensor(ktok, scr1, scr3, OP.max))
                            pvV.wait(v)
                            V(v, v.tensor_tensor(vtok, pV[par], bvb[:, sw * KSW:(sw + 1) * KSW], OP.add))
                        blk.vector(f_phi_v)
                        c_tok = pvV.n
                        c_max_prev = pvV.n - 1
                        pKfree[par] = (c_phi_a, c_tok)

                        def f_kvmm(t, idx=idx, ct=c_tok, kvf=kvps_free, ktok=ktok, vtok=vtok):
                            pvV.wait(t, max(ct, kvf))
                            for h in range(NKVS):
                                ks = ktok[:, h * 128:(h + 1) * 128]
                                vs = vtok[:, h * 128:(h + 1) * 128]
                                t.matmul(pKV[:, h * 128:(h + 1) * 128], ks, vs,
                                         start=(idx == 0 and h == 0),
                                         stop=(idx == NTB - 1 and h == NKVS - 1),
                                         skip_group_check=True)
                                last = t.matmul(pKS[:, h:h + 1], ks, ones_bf[:, 0:1],
                                                start=(idx == 0 and h == 0),
                                                stop=(idx == NTB - 1 and h == NKVS - 1),
                                                skip_group_check=True)
                            gT.inc(last)
                        blk.tensor(f_kvmm)
                        ktok_free[idx % 2] = gT.n

                    def f_kvev(v, sw=sw, c=gT.n):
                        gT.wait(v, c)
                        V(v, v.tensor_copy(kvt[:, sw * NKVS * 128:(sw + 1) * NKVS * 128], pKV))
                        pvV.wait(v)
                        V(v, v.tensor_scalar(scrK, pKS, 1e-10, None, OP.add))
                        pvV.wait(v)
                        V(v, v.reciprocal(rnorm[:, sw * NKVS:(sw + 1) * NKVS], scrK))
                    blk.vector(f_kvev)
                    kvps_free = pvV.n

        PH2 = ExitStack()
        Sp2 = lambda n_, sh, dty: PH2.enter_context(nc.sbuf_tensor(n_, sh, dty)).ap()
        with PH2:
            atnT = Sp2("atnT", [128, NH * TO], BF16)
            with ExitStack() as PH:
                Sp = lambda n_, sh, dty: PH.enter_context(nc.sbuf_tensor(n_, sh, dty)).ap()
                wqbf = [Sp(f"wqbf{i}", [128, DT * 128], BF16) for i in range(2)]
                qbf = Sp("qbf", [128, TO], BF16)
                qs1 = Sp("qs1", [128, TO], F32)
                qs2 = Sp("qs2", [128, TO], F32)
                qs3 = Sp("qs3", [128, TO], F32)
                pQ = [PH.enter_context(nc.psum_tensor(f"pQ{i}", [128, TO], F32)).ap() for i in range(2)]
                pA = [PH.enter_context(nc.psum_tensor(f"pA{i}", [128, TO], F32)).ap() for i in range(2)]

                with nc.Block("p2") as blk:
                    def f_b(g):
                        SBQ.inc(g.dma_start(out=bqc, in_=bqD), 16)
                    blk.gpsimd(f_b)

                    qgrp = [0] * NH
                    agrp = [0] * NH
                    pQfree = [(0, 0)] * 2
                    pAfree = [0] * 2
                    qbf_free = 0
                    qs2_free = 0
                    qs1_prev = 0

                    for h in range(NH):
                        par = h % 2

                        def f_w(g, h=h, par=par, free=(qgrp[h - 2] if h >= 2 else 0)):
                            gT.wait(g, free)
                            SQ[par].inc(g.dma_start(out=wqbf[par], in_=wq[h].rearrange("p t f -> p (t f)")), 16)
                        blk.gpsimd(f_w)
                        w_rdy = SQ[par].n

                        def f_q(t, h=h, par=par, w_rdy=w_rdy, slot=pQfree[par]):
                            SQ[par].wait(t, w_rdy)
                            SBQ.wait(t, 16)
                            t.wait_ge(pvA.h, slot[0])
                            t.wait_ge(pvV.h, slot[1])
                            for qc in range(NQT):
                                for d in range(DT):
                                    lhs = wqbf[par][:, d * 128:(d + 1) * 128]
                                    last = t.matmul(pQ[par][:, qc * 512:(qc + 1) * 512], lhs,
                                                    xnt[:, d * TO + qc * 512: d * TO + (qc + 1) * 512],
                                                    start=(d == 0), stop=(d == DT - 1))
                            gT.inc(last)
                        blk.tensor(f_q)
                        qgrp[h] = gT.n

                        def f_phi_v1(v, h=h, par=par, qg=qgrp[h], ps1=qs1_prev):
                            gT.wait(v, qg)
                            SBQ.wait(v, 16)
                            pvA.wait(v, ps1)
                            pvV.wait(v)
                            V(v, v.tensor_scalar(qs3, pQ[par], bqc[:, h:h + 1], 1.0,
                                                 OP.add, OP.add))
                        blk.vector(f_phi_v1)
                        c_s3 = pvV.n

                        def f_phi_a(s, cs3=c_s3, q2f=qs2_free):
                            pvV.wait(s, cs3)
                            pvV.wait(s, q2f)
                            A(s, s.activation(qs1, qs3, AF.Relu, scale=-1.0, bias=1.0))
                            pvA.wait(s)
                            A(s, s.activation(qs2, qs1, AF.Exp, scale=-1.0))
                        blk.scalar(f_phi_a)
                        c_phi_a = pvA.n
                        qs1_prev = pvA.n - 1

                        def f_phi_v(v, h=h, par=par, ca=c_phi_a, qbff=qbf_free):
                            pvA.wait(v, ca)
                            gT.wait(v, qbff)
                            pvV.wait(v)
                            V(v, v.tensor_tensor(qbf, qs2, qs3, OP.max))
                        blk.vector(f_phi_v)
                        c_q = pvV.n
                        qs2_free = pvV.n
                        pQfree[par] = (c_phi_a, c_q)

                        def f_att(t, h=h, par=par, cq=c_q, paf=pAfree[par]):
                            pvV.wait(t, max(cq, paf))
                            g = h // GQ
                            for qc in range(NQT):
                                last = t.matmul(pA[par][:, qc * 512:(qc + 1) * 512],
                                                kvt[:, g * 128:(g + 1) * 128],
                                                qbf[:, qc * 512:(qc + 1) * 512],
                                                start=True, stop=True)
                            gT.inc(last)
                        blk.tensor(f_att)
                        agrp[h] = gT.n
                        qbf_free = gT.n

                        def f_nrm(v, h=h, par=par, ag=agrp[h]):
                            gT.wait(v, ag)
                            V(v, v.tensor_scalar(atnT[:, h * TO:(h + 1) * TO], pA[par],
                                                 rnorm[:, (h // GQ):(h // GQ) + 1], None, OP.mult))
                        blk.vector(f_nrm)
                        pAfree[par] = pvV.n

            with ExitStack() as PH:
                Sp = lambda n_, sh, dty: PH.enter_context(nc.sbuf_tensor(n_, sh, dty)).ap()
                wobf = [Sp(f"wobf{i}", [128, DT * OCH], BF16) for i in range(2)]
                ostg = Sp("ostg", [128, NTO * OCH], BF16)
                bor = Sp("bor", [1, D], BF16)
                pO = [PH.enter_context(nc.psum_tensor(f"pO{i}", [128, OCH], F32)).ap() for i in range(4)]

                with nc.Block("p3") as blk:
                    def f_b(g):
                        SBO.inc(g.dma_start(out=bor, in_=boD), 16)
                    blk.gpsimd(f_b)

                    ogrp = {}
                    pOfree = [0] * 4
                    store = [0] * NOCH

                    for ch in range(NOCH):
                        par = ch % 2

                        PCo = DT // 4
                        if ch == 0:
                            def f_w(g):
                                for p in range(4):
                                    SOP[p].inc(g.dma_start(
                                        out=wobf[0][:, p * PCo * OCH:(p + 1) * PCo * OCH],
                                        in_=wo[0][:, p * PCo:(p + 1) * PCo, :].rearrange("p t f -> p (t f)")), 16)
                            blk.gpsimd(f_w)
                            w_rdy = [SOP[p].n for p in range(4)]
                        else:
                            def f_w(g, ch=ch, par=par,
                                    free=(ogrp[(ch - 2, NTO - 1)] if ch >= 2 else 0)):
                                gT.wait(g, free)
                                SO[par].inc(g.dma_start(out=wobf[par], in_=wo[ch].rearrange("p t f -> p (t f)")), 16)
                            blk.gpsimd(f_w)
                            w_rdy = SO[par].n

                        for tau in range(NTO):
                            k4 = (ch * NTO + tau) % 4

                            def f_o(t, ch=ch, tau=tau, par=par, k4=k4, w_rdy=w_rdy,
                                    pf=pOfree[k4], PCo=PCo):
                                if ch > 0:
                                    SO[par].wait(t, w_rdy)
                                SBO.wait(t, 16)
                                t.wait_ge(pvA.h, pf)
                                for i in range(DT):
                                    if ch == 0 and i % PCo == 0:
                                        SOP[i // PCo].wait(t, w_rdy[i // PCo])
                                    t.matmul(pO[k4], atnT[:, i * TO + tau * 128: i * TO + (tau + 1) * 128],
                                             wobf[par][:, i * OCH:(i + 1) * OCH],
                                             start=(i == 0), stop=False)
                                gT.inc(t.matmul(pO[k4], ones_bf[0:1, 0:128],
                                                bor[0:1, ch * OCH:(ch + 1) * OCH],
                                                start=False, stop=True))
                            blk.tensor(f_o)
                            ogrp[(ch, tau)] = gT.n

                            def f_ev(s, ch=ch, tau=tau, k4=k4, og=ogrp[(ch, tau)],
                                     stf=(store[ch - 1] if ch >= 1 else 0)):
                                gT.wait(s, og)
                                if tau == 0:
                                    SP3.wait(s, stf)
                                A(s, s.copy(ostg[:, tau * OCH:(tau + 1) * OCH], pO[k4]))
                            blk.scalar(f_ev)
                            pOfree[k4] = pvA.n

                        def f_st(s, ch=ch, ev=pvA.n):
                            pvA.wait(s, ev)
                            SP3.inc(s.dma_start(out=ascr[ch].rearrange("p t f -> p (t f)"), in_=ostg), 16)
                        blk.scalar(f_st)
                        store[ch] = SP3.n

        with ExitStack() as PH:
            Sp = lambda n_, sh, dty: PH.enter_context(nc.sbuf_tensor(n_, sh, dty)).ap()
            webf = [Sp(f"webf{i}", [128, DT * HCH], BF16) for i in range(2)]
            bb_sb = [Sp(f"bb_sb{i}", [128, HCH], BF16) for i in range(2)]
            acc = Sp("acc", [128, NTO * HCH], F32)
            atns = Sp("atns", [128, NTO * HCH], BF16)
            ostg5 = [Sp(f"ostg5{i}", [128, NTO * HCH], F32) for i in range(2)]
            trl = [Sp(f"trl{i}", [128, HCH], F32) for i in range(2)]
            ber = Sp("ber", [1, E * HCH], BF16)
            pE = [PH.enter_context(nc.psum_tensor(f"pE{i}", [128, HCH], F32)).ap() for i in range(4)]
            pBB = PH.enter_context(nc.psum_tensor("pBB", [128, HCH], F32)).ap()
            HOC = HCH // OCH

            with nc.Block("p5") as blk:
                p3_done = SP3.n

                def f_pro_s(s):
                    SP3.wait(s, p3_done)
                blk.sync(f_pro_s)

                def f_pro_a(s):
                    SP3.wait(s, p3_done)
                blk.scalar(f_pro_a)

                def f_pro_g(g):
                    SP3.wait(g, p3_done)
                blk.gpsimd(f_pro_g)

                egrp = {}
                bbev = {}
                bb_free = [0] * 2
                pEfree = [0] * 4
                trl_free = [0] * 2
                outst = [0] * NCH
                combine_done = 0
                ber_free = 0

                for ch in range(NCH):
                    spar = ch % 2

                    def f_x(s, ch=ch, spar=spar, of=(outst[ch - 2] if ch >= 2 else 0)):
                        SOUT[spar].wait(s, of)
                        SXF[spar].inc(s.dma_start(out=ostg5[spar], in_=xoc[ch].rearrange("p t f -> p (t f)")), 16)
                    blk.sync(f_x)
                    xfill = SXF[spar].n

                    def f_at(s, ch=ch, cd=combine_done):
                        pvV.wait(s, cd)
                        for oc in range(HOC):
                            SAT.inc(s.dma_start(
                                out=atns.rearrange("p (t f) -> p t f", t=NTO)[:, :, oc * OCH:(oc + 1) * OCH],
                                in_=ascr[ch * HOC + oc]), 16)
                    blk.scalar(f_at)
                    atld = SAT.n

                    def f_be(g, ch=ch, bf=ber_free):
                        gT.wait(g, bf)
                        SBE.inc(g.dma_start(out=ber, in_=beD[ch:ch + 1, :]), 16)
                    blk.gpsimd(f_be)
                    ber_rdy = SBE.n

                    for e in range(E):
                        wpar = e % 2
                        k2 = ch * E + e

                        PCe = DT // 4
                        if k2 == 0:
                            def f_w(g):
                                for p in range(4):
                                    SEP[p].inc(g.dma_start(
                                        out=webf[0][:, p * PCe * HCH:(p + 1) * PCe * HCH],
                                        in_=we[0, 0][:, p * PCe:(p + 1) * PCe, :].rearrange("p t f -> p (t f)")), 16)
                            blk.gpsimd(f_w)
                            w_rdy = [SEP[p].n for p in range(4)]
                        else:
                            def f_w(g, ch=ch, e=e, wpar=wpar,
                                    free=(egrp[(ch * E + e - 2, NTO - 1)] if k2 >= 2 else 0)):
                                gT.wait(g, free)
                                SE[wpar].inc(g.dma_start(out=webf[wpar], in_=we[ch, e].rearrange("p t f -> p (t f)")), 16)
                            blk.gpsimd(f_w)
                            w_rdy = SE[wpar].n

                        def f_bb(t, ch=ch, e=e, ber_rdy=ber_rdy, bbe=(bbev.get((ch * E + e - 1,), 0))):
                            SBE.wait(t, ber_rdy)
                            t.wait_ge(pvA.h, bbe)
                            gT.inc(t.matmul(pBB, ones_bf[0:1, 0:128],
                                            ber[0:1, e * HCH:(e + 1) * HCH], start=True, stop=True))
                        blk.tensor(f_bb)
                        c_bb = gT.n
                        if e == E - 1:
                            ber_free_new = c_bb

                        def f_bbe(s, ch=ch, e=e, wpar=wpar, cb=c_bb, bf=bb_free[wpar]):
                            gT.wait(s, cb)
                            pvV.wait(s, bf)
                            A(s, s.copy(bb_sb[wpar], pBB))
                        blk.scalar(f_bbe)
                        bbev[(ch * E + e,)] = pvA.n
                        bb_rdy = pvA.n

                        for tau in range(NTO):
                            k4 = (k2 * NTO + tau) % 4

                            def f_e(t, ch=ch, e=e, tau=tau, wpar=wpar, k4=k4, k2=k2,
                                    w_rdy=w_rdy, pf=pEfree[k4], PCe=PCe):
                                if k2 > 0:
                                    SE[wpar].wait(t, w_rdy)
                                t.wait_ge(pvV.h, pf)
                                for d in range(DT):
                                    if k2 == 0 and d % PCe == 0:
                                        SEP[d // PCe].wait(t, w_rdy[d // PCe])
                                    last = t.matmul(pE[k4],
                                                    xnt[:, d * TO + tau * 128: d * TO + (tau + 1) * 128],
                                                    webf[wpar][:, d * HCH:(d + 1) * HCH],
                                                    start=(d == 0), stop=(d == DT - 1))
                                gT.inc(last)
                            blk.tensor(f_e)
                            egrp[(k2, tau)] = gT.n

                            def f_r(v, ch=ch, e=e, tau=tau, k4=k4, wpar=wpar, eg=egrp[(k2, tau)],
                                    cd=combine_done, tf=trl_free[tau % 2], br=bb_rdy):
                                gT.wait(v, eg)
                                pvA.wait(v, br)
                                if e == 0 and tau == 0:
                                    pvV.wait(v, max(cd, mask_done))
                                pvV.wait(v, tf)
                                V(v, v.tensor_tensor(trl[tau % 2], pE[k4], bb_sb[wpar], OP.add))
                                pvV.wait(v)
                                if e == 0:
                                    V(v, v.tensor_scalar(acc[:, tau * HCH:(tau + 1) * HCH],
                                                         trl[tau % 2],
                                                         mask[:, tau * E:tau * E + 1], 0.0,
                                                         OP.mult, OP.max))
                                else:
                                    V(v, v.tensor_scalar(trl[tau % 2], trl[tau % 2],
                                                         mask[:, tau * E + e:tau * E + e + 1], 0.0,
                                                         OP.mult, OP.max))
                                    pvV.wait(v)
                                    V(v, v.tensor_tensor(acc[:, tau * HCH:(tau + 1) * HCH],
                                                         trl[tau % 2],
                                                         acc[:, tau * HCH:(tau + 1) * HCH], OP.add))
                            blk.vector(f_r)
                            pEfree[k4] = pvV.n
                            trl_free[tau % 2] = pvV.n
                            if tau == NTO - 1:
                                bb_free[wpar] = pvV.n

                            if e == E - 1:
                                def f_cmb(v, ch=ch, tau=tau, spar=spar, xfill=xfill, atld=atld):
                                    SXF[spar].wait(v, xfill)
                                    SAT.wait(v, atld)
                                    sl = slice(tau * HCH, (tau + 1) * HCH)
                                    pvV.wait(v)
                                    V(v, v.tensor_tensor(trl[tau % 2], acc[:, sl], atns[:, sl], OP.add))
                                    pvV.wait(v)
                                    V(v, v.tensor_tensor(ostg5[spar][:, sl], trl[tau % 2],
                                                         ostg5[spar][:, sl], OP.add))
                                blk.vector(f_cmb)
                                trl_free[tau % 2] = pvV.n
                    ber_free = ber_free_new
                    combine_done = pvV.n

                    def f_out(s, ch=ch, spar=spar, cd=combine_done):
                        pvV.wait(s, cd)
                        SOUT[spar].inc(s.dma_start(
                            out=out.rearrange("(t p) n -> p t n", p=128)[:, :, ch * HCH:(ch + 1) * HCH],
                            in_=ostg5[spar].rearrange("p (t f) -> p t f", t=NTO)), 16)
                    blk.sync(f_out)
                    outst[ch] = SOUT[spar].n

                def f_fin(s):
                    for chn in [SOUT[0], SOUT[1], SXF[0], SXF[1], SXA[0], SXA[1],
                                SXB[0], SXB[1], SAT, SP3,
                                SPRE, SKV, SQ[0], SQ[1], SBQ, SO[0], SO[1], SBO,
                                SE[0], SE[1], SBE,
                                SKP[0], SKP[1], SKP[2], SKP[3], SVP[0], SVP[1], SVP[2], SVP[3],
                                SOP[0], SOP[1], SOP[2], SOP[3], SEP[0], SEP[1], SEP[2], SEP[3]]:
                        chn.wait(s)
                blk.sync(f_fin)

    return nc


def kernel(**inputs):
    c = mk_cfg()
    D, TO, DT, NTO = c["D"], c["TO"], c["DT"], c["NTO"]
    NCH, HCH, NOCH, OCH, KSW, NH, E, KD = (c["NCH"], c["HCH"], c["NOCH"], c["OCH"],
                                           c["KSW"], c["NH"], c["E"], c["KD"])
    f32 = np.float32
    Wq = np.asarray(inputs["L0_attn_Wq"], f32)
    Wk = np.asarray(inputs["L0_attn_Wk"], f32)
    Wv = np.asarray(inputs["L0_attn_Wv"], f32)
    Wo = np.asarray(inputs["L0_attn_Wo"], f32)
    We = np.asarray(inputs["L0_ffn_W_e"], f32)
    Wg = np.asarray(inputs["L0_ffn_W_gate"], f32)

    wq_h = np.ascontiguousarray(Wq.T.reshape(DT, 128, NH, 128).transpose(2, 1, 0, 3))
    wk_h = np.ascontiguousarray(Wk.T.reshape(DT, 128, 2, KSW).transpose(2, 1, 0, 3))
    wv_h = np.ascontiguousarray(Wv.T.reshape(DT, 128, 2, KSW).transpose(2, 1, 0, 3))
    wo_h = np.ascontiguousarray(Wo.T.reshape(DT, 128, NOCH, OCH).transpose(2, 1, 0, 3))
    we_h = np.ascontiguousarray(We.reshape(E, DT, 128, NCH, HCH).transpose(3, 0, 2, 1, 4))
    wg_h = np.ascontiguousarray(Wg.reshape(DT, 128, E).transpose(1, 0, 2))
    bq_h = np.ascontiguousarray(np.asarray(inputs["L0_attn_bq"], f32).reshape(NH, 128).T)
    bk_h = np.asarray(inputs["L0_attn_bk"], f32).reshape(1, KD)
    bv_h = np.asarray(inputs["L0_attn_bv"], f32).reshape(1, KD)
    bo_h = np.asarray(inputs["L0_attn_bo"], f32).reshape(1, D)
    bg_h = np.asarray(inputs["L0_ffn_b_gate"], f32).reshape(E, 1)
    be_h = np.ascontiguousarray(
        np.asarray(inputs["L0_ffn_b_e"], f32).reshape(E, NCH, HCH).transpose(1, 0, 2)
    ).reshape(NCH, E * HCH)

    x = np.asarray(inputs["L0_attn_hidden_states"], f32)
    nc = build_program(c)

    in_maps = []
    for core in range(8):
        b, half = core // 2, core % 2
        xown = np.ascontiguousarray(x[b, half * TO:(half + 1) * TO, :])
        xoth = np.ascontiguousarray(x[b, (1 - half) * TO:(2 - half) * TO, :])
        xoc = np.ascontiguousarray(xown.reshape(NTO, 128, NCH, HCH).transpose(2, 1, 0, 3))
        in_maps.append({
            "xo": xown, "xoc": xoc, "xb": xoth,
            "wq": wq_h, "wk": wk_h, "wv": wv_h, "wo": wo_h, "we": we_h, "wg": wg_h,
            "bq": bq_h, "bk": bk_h, "bv": bv_h, "bo": bo_h, "bg": bg_h, "be": be_h,
        })

    res = run_bass_kernel_spmd(nc, in_maps, core_ids=list(range(8)))
    full = np.empty((4, 2048, 4096), dtype=np.float32)
    for core in range(8):
        b, half = core // 2, core % 2
        full[b, half * TO:(half + 1) * TO, :] = res.results[core]["out"]
    return full
